# version 31
# baseline (speedup 1.0000x reference)
import numpy as np
import ml_dtypes
import concourse.bass as bass
import bass_rust
import concourse.mybir as mybir
from concourse.bass_utils import run_bass_kernel_spmd

F32 = mybir.dt.float32
BF16 = mybir.dt.bfloat16
AF = mybir.ActivationFunctionType
ALU = mybir.AluOpType
AX = mybir.AxisListType

T = 2048
C = 1024
NT = 16
CH = 8
H = 16
HD = 64
NO = 44
OFF = list(range(33)) + [48, 64, 96, 128, 192, 256, 384, 512, 768, 1024, 1536]
INV = 0.125
NEG = -1.0e30
E = 16
FHD = 128
COLW = NO * 256


TRACE = {}
STOP_AFTER = None


class _StopEmit(Exception):
    pass


def tt_min(o):
    return max(0, (o - 127 + 127) // 128)


def build_nc(debug=False):
    nc = bass.Bass("TRN2", target_bir_lowering=False)
    d = {}

    def din(name, shape, dt):
        d[name] = nc.dram_tensor(name, list(shape), dt, kind="ExternalInput").ap()

    def dint(name, shape, dt):
        d[name] = nc.dram_tensor(name, list(shape), dt, kind="Internal").ap()

    din("x", (T, C), F32)
    din("qkv_wT", (C, 3 * C), F32)
    din("qw_q", (C, C), F32)
    din("gate_wT", (C, C), F32)
    din("out_wT", (C, C), F32)
    din("qb_fm", (128, 16), F32)
    din("vb_row", (1, C), F32)
    din("qbh", (64, 16), F32)
    din("gb_row", (1, C), F32)
    din("ob_row", (1, C), F32)
    din("se", (44, 64), F32)
    din("pb_row", (1, COLW), F32)
    din("ifg_row", (1, 16), F32)
    din("sim", (C, E), F32)
    din("gates_row", (1, E), F32)
    din("fq", (E, C, FHD), F32)
    din("fk", (E, C, FHD), F32)
    din("fv", (E, C, FHD), F32)
    din("fo", (E, FHD, C), F32)
    din("IDN", (128, 128), BF16)
    din("SPMASK", (128, COLW), BF16)
    din("CM4", (128, 4 * 512), F32)

    d["y"] = nc.dram_tensor("y", [T, C], F32, kind="ExternalOutput").ap()
    if debug:
        for nm, shape, dt in (("dbg_t1", (128, COLW), BF16), ("dbg_v", (T, C), BF16),
                              ("dbg_gate", (T, C), BF16), ("dbg_qfm", (128, 4096), BF16),
                              ("dbg_qk", (128, COLW), BF16),
                              ("dbg_lg", (128, COLW), F32), ("dbg_pm", (128, COLW), BF16),
                              ("dbg_pf", (128, COLW), BF16), ("dbg_x2", (T, C), F32),
                              ("dbg_rw", (128, NT * E), F32), ("dbg_wz", (128, NT * E), F32),
                              ("dbg_zc", (128, NT), F32), ("dbg_rn", (128, NT), F32),
                              ("dbg_of", (T, C), BF16),
                              ("dbg_z", (128, NT * 16), F32), ("dbg_zr", (128, NT * 16), F32),
                              ("dbg_w2", (128, NT * 16), F32), ("dbg_ifg", (128, 16), F32),
                              ("dbg_wse", (128, 5632), BF16), ("dbg_wtall", (1024, 4096), BF16),
                              ("dbg_wvg", (128, 16384), BF16)):
            d[nm] = nc.dram_tensor(nm, list(shape), dt, kind="ExternalOutput").ap()

    dint("wTallD", (C, 4096), BF16)
    dint("outwD", (C, C), BF16)
    dint("wQbfD", (E, C, 128), BF16)
    dint("wKbfD", (E, C, 128), BF16)
    dint("wVbfD", (E, C, 128), BF16)
    dint("oEbfD", (E, FHD, C), BF16)
    dint("PBbcD", (128, COLW), F32)
    dint("gateD", (T, C), BF16)
    dint("vD", (1536 + T, C), BF16)
    dint("x2D", (T, C), F32)
    dint("qsbD", (44, 16), F32)
    dint("snD", (1, E), F32)
    dint("zD", (1, T), F32)

    BIG = nc.alloc_sbuf_tensor("BIG", [128, 49152], BF16)
    SCR = nc.alloc_sbuf_tensor("SCR", [128, 16384], BF16)
    qk_all = nc.alloc_sbuf_tensor("qk_all", [128, COLW], BF16)
    t1_all = nc.alloc_sbuf_tensor("t1_all", [128, COLW], BF16)
    xt = nc.alloc_sbuf_tensor("xt", [128, 3072], F32)
    xnt = nc.alloc_sbuf_tensor("xnt", [128, 2 * C], BF16)
    idn = nc.alloc_sbuf_tensor("idn", [128, 128], BF16)
    h2 = nc.alloc_sbuf_tensor("h2", [128, CH * 16], BF16)
    ones_b = nc.alloc_sbuf_tensor("ones_b", [1, 128], BF16)
    ones_f = nc.alloc_sbuf_tensor("ones_f", [1, 128], F32)
    onecol_b = nc.alloc_sbuf_tensor("onecol_b", [128, 1], BF16)
    qbfm_s = nc.alloc_sbuf_tensor("qbfm_s", [128, 16], F32)
    vbrow = nc.alloc_sbuf_tensor("vbrow", [1, C], BF16)
    gbrow = nc.alloc_sbuf_tensor("gbrow", [1, C], BF16)
    obrow = nc.alloc_sbuf_tensor("obrow", [1, C], BF16)
    qsbrow = nc.alloc_sbuf_tensor("qsbrow", [1, 704], BF16)
    seb = nc.alloc_sbuf_tensor("seb", [128, 128], BF16)
    searr = nc.alloc_sbuf_tensor("searr", [128, 44], BF16)
    searr2 = nc.alloc_sbuf_tensor("searr2", [128, 88], BF16)
    qbh_s = nc.alloc_sbuf_tensor("qbh_s", [64, 16], BF16)
    ifgbc = nc.alloc_sbuf_tensor("ifgbc", [128, 16], F32)
    simb = nc.alloc_sbuf_tensor("simb", [128, CH * E], BF16)
    sim2 = nc.alloc_sbuf_tensor("sim2", [128, CH * E], F32)
    snrow = nc.alloc_sbuf_tensor("snrow", [1, E], F32)
    sgrow = nc.alloc_sbuf_tensor("sgrow", [1, E], F32)
    snbc = nc.alloc_sbuf_tensor("snbc", [128, E], F32)
    sgbc = nc.alloc_sbuf_tensor("sgbc", [128, E], F32)
    rn_all = nc.alloc_sbuf_tensor("rn_all", [128, NT], F32)
    rw_all = nc.alloc_sbuf_tensor("rw_all", [128, NT * E], F32)
    wz_all = nc.alloc_sbuf_tensor("wz_all", [128, NT * E], F32)
    zcol = nc.alloc_sbuf_tensor("zcol", [128, NT], F32)
    zrcol = nc.alloc_sbuf_tensor("zrcol", [128, NT], F32)
    stat = nc.alloc_sbuf_tensor("stat", [128, 32], F32)
    z_all = nc.alloc_sbuf_tensor("z_all", [128, NT * 16], F32)
    zr_all = nc.alloc_sbuf_tensor("zr_all", [128, NT * 16], F32)
    w2_all = nc.alloc_sbuf_tensor("w2_all", [128, NT * 16], F32)
    qsb44 = nc.alloc_sbuf_tensor("qsb44", [44, 16], F32)
    rowstage = nc.alloc_sbuf_tensor("rowstage", [128, 1024], F32)

    pm8 = nc.alloc_psum_tensor("pm8", [128, 3072], F32)
    pt1 = nc.alloc_psum_tensor("pt1", [128, 1024], BF16)
    pt2 = nc.alloc_psum_tensor("pt2", [128, 1024], BF16)

    def pt_(c):
        t_ = pt1 if (c % 2 == 0) else pt2
        return t_[:, (c // 2 % 4) * 128: (c // 2 % 4) * 128 + 128]

    def bank(k, w=512):
        return pm8[:, k * 512: k * 512 + w]

    semnames = ["s_dma", "s_dmw", "s_dvs", "s_dv2", "s_gp", "s_pe", "s_dve", "s_act", "s_pool", "s_st"]
    sems = {n: nc.alloc_semaphore(n) for n in semnames}

    def xnfm(c):
        return BIG[:, 32768 + c * 2048: 32768 + (c + 1) * 2048]

    def Qfm(c):
        return BIG[:, c * 2048: (c + 1) * 2048]

    def Kfm(c):
        return BIG[:, 16384 + c * 2048: 16384 + (c + 1) * 2048]

    def Kodd(c):
        return BIG[:, 32768 + c * 2048: 32768 + (c + 1) * 2048]

    def convf32():
        return BIG[:, 0:6144].bitcast(F32)

    def convbf():
        return BIG[:, 6144: 9216]

    def wse(c):
        return SCR[:, c * 704: (c + 1) * 704]

    def wcol(c):
        return SCR[:, 5632 + c * 256: 5632 + (c + 1) * 256]

    def wvg(c):
        if c < 5:
            return qk_all[:, c * 2048: (c + 1) * 2048]
        return SCR[:, 5632 + (c - 5) * 2048: 5632 + (c - 4) * 2048]

    def vstage():
        return SCR[:, 11776: 12800]

    def gstage():
        return SCR[:, 12800: 13824]

    def tmpc(c):
        return SCR[:, c * 2048: (c + 1) * 2048]

    def lgf32():
        return BIG[:, 0: 2 * COLW].bitcast(F32)

    def pbbuf(pi):
        return SCR[:, pi * 2816: (pi + 1) * 2816].bitcast(F32)

    def spbuf(pi):
        return SCR[:, 5632 + pi * 2816: 5632 + (pi + 1) * 2816]

    def outwT(c):
        return BIG[:, 24576 + c * 1024: 24576 + (c + 1) * 1024]

    def vsbuf(pi):
        return BIG[:, 16384 + pi * 1024: 16384 + (pi + 1) * 1024]

    def prodbuf(pi):
        return SCR[:, 4096 + pi * 1024: 4096 + (pi + 1) * 1024] if pi < 4 else \
            BIG[:, 22528 + (pi - 4) * 1024: 22528 + (pi - 3) * 1024]

    def ofbuf():
        return SCR[:, 8192: 9216]

    def gtbuf():
        return SCR[:, 9216: 10240]

    def ogbuf():
        return SCR[:, 10240: 11264]

    def ogfm():
        return SCR[:, 11264: 12288]

    def Vres(tt):
        return BIG[:, tt * 1024: (tt + 1) * 1024]

    def xn2fm(c):
        return BIG[:, c * 2048: (c + 1) * 2048]

    def expS(j):
        return BIG[:, 16384 + j * 512: 16384 + (j + 1) * 512]

    def scfm(qi, e):
        return BIG[:, 24576 + (qi * 16 + e) * 128: 24576 + (qi * 16 + e + 1) * 128]

    V2 = lambda qt: BIG[:, 32768 + qt * FHD: 32768 + (qt + 1) * FHD]
    q2fm = BIG[:, 34816: 36864]
    k2fm = BIG[:, 36864: 38912]
    aofm = BIG[:, 38912: 40960]
    aotok = lambda qt: BIG[:, 40960 + qt * FHD: 40960 + (qt + 1) * FHD]

    def webuf(pi):
        return SCR[:, pi * 3072: (pi + 1) * 3072]

    def rawsb(pi):
        return SCR[:, 9216 + pi * 384: 9216 + (pi + 1) * 384]

    def dgbuf(pi):
        return SCR[:, 9984 + pi * 128: 9984 + (pi + 1) * 128]

    def qkscr():
        return SCR[:, 10496: 10752]

    def causal():
        return SCR[:, 8192: 12288].bitcast(F32)

    def zrow():
        return SCR[0:1, 12288: 16384].bitcast(F32)

    def oebuf(pi):
        return SCR[:, pi * 1024: (pi + 1) * 1024]

    def lnsq():
        return xnt[:, :]

    def p7f(k):
        return SCR[:, 14000 + k * 32: 14000 + (k + 1) * 32].bitcast(F32)

    def walk(role, eng):
        cnt = {n: 0 for n in semnames}
        tr = TRACE.setdefault(role, [])

        def ckpt(name):
            if STOP_AFTER == name:
                raise _StopEmit()

        def tick(sem, inst, n=1):
            cnt[sem] += n
            if inst is not None:
                inst.then_inc(sems[sem], n)

        def on(r):
            return role == r

        def wt(sem, r):
            tr.append((sem, r, cnt[sem]))
            if role == r:
                eng.wait_ge(sems[sem], cnt[sem])

        def wtv(sem, r, val):
            tr.append((sem, r, val))
            if role == r and val > 0:
                eng.wait_ge(sems[sem], val)

        def vstep(emit):
            wt("s_dve", "vector")
            if on("vector"):
                tick("s_dve", emit(eng), 1)
            else:
                tick("s_dve", None, 1)

        def dma(ring, sem, out, in_, **kw):
            if on(ring):
                tick(sem, eng.dma_start(out=out, in_=in_, **kw), 16)
            else:
                tick(sem, None, 16)

        try:
            if on("sync"):
                tick("s_dma", eng.dma_start(out=idn[:, :], in_=d["IDN"]), 16)
                tick("s_dma", eng.dma_start(out=qbfm_s[:, :], in_=d["qb_fm"]), 16)
            else:
                tick("s_dma", None, 32)

            if on("gpsimd"):
                eng.memset(ones_b[:, :], 1.0)
                eng.memset(ones_f[:, :], 1.0)
                eng.memset(onecol_b[:, :], 1.0)
                eng.memset(seb[:, :], 0.0)
                eng.memset(searr2[:, :], 0.0)
                eng.memset(SCR[:, 15872:16384], 0.0)
                eng.memset(h2[:, :], 0.0)
                last = None
                for c in range(CH):
                    eng.memset(h2[0:64, c * 16 + 2 * c: c * 16 + 2 * c + 1], 1.0)
                    last = eng.memset(h2[64:128, c * 16 + 2 * c + 1: c * 16 + 2 * c + 2], 1.0)
                tick("s_pool", last, 1)
            else:
                tick("s_pool", None, 1)

            for nm, dst, w in (("vb_row", vbrow, C), ("gb_row", gbrow, C), ("ob_row", obrow, C)):
                dma("sync", "s_dma", rowstage[0:1, 0:w], d[nm])
                wt("s_dma", "scalar")
                if on("scalar"):
                    tick("s_act", eng.copy(dst[:, :], rowstage[0:1, 0:w]), 1)
                else:
                    tick("s_act", None, 1)
                wt("s_act", "sync")
            dma("sync", "s_dma", rowstage[0:64, 0:16], d["qbh"])
            wt("s_dma", "scalar")
            if on("scalar"):
                tick("s_act", eng.copy(qbh_s[:, :], rowstage[0:64, 0:16]), 1)
            else:
                tick("s_act", None, 1)
            wt("s_act", "sync")
            dma("sync", "s_dma", rowstage[0:44, 0:64], d["se"])
            wt("s_dma", "scalar")
            wt("s_pool", "scalar")
            if on("scalar"):
                eng.copy(seb[0:44, 0:64], rowstage[0:44, 0:64])
                tick("s_act", eng.copy(seb[0:44, 64:128], rowstage[0:44, 0:64]), 1)
            else:
                tick("s_act", None, 1)
            wt("s_act", "tensor")
            wt("s_dma", "tensor")
            if on("tensor"):
                tick("s_pe", eng.transpose(pt1[:, 0:128], seb[:, :], idn[:, :]), 1)
            else:
                tick("s_pe", None, 1)
            wt("s_pe", "vector")
            if on("vector"):
                tick("s_dve", eng.tensor_copy(searr[:, :], pt1[:, 0:44]), 1)
            else:
                tick("s_dve", None, 1)
            snap_searr_cp = cnt["s_dve"]
            wt("s_dve", "scalar")
            wt("s_pool", "scalar")
            if on("scalar"):
                eng.copy(searr2[0:64, 0:44], searr[0:64, :])
                tick("s_act", eng.copy(searr2[64:128, 44:88], searr[64:128, :]), 1)
            else:
                tick("s_act", None, 1)

            for c in range(CH):
                wt("s_dve", "sync")
                dma("sync", "s_dma", rowstage[:, 0:16], d["sim"][c * 128:(c + 1) * 128, :])
                wt("s_dma", "vector")
                if on("vector"):
                    eng.tensor_copy(simb[:, c * E:(c + 1) * E], rowstage[:, 0:16])
                    tick("s_dve", eng.tensor_tensor(sim2[:, c * E:(c + 1) * E], rowstage[:, 0:16],
                                                    rowstage[:, 0:16], ALU.mult), 1)
                else:
                    tick("s_dve", None, 1)

            wt("s_dve", "sync")
            dma("sync", "s_dma", rowstage[0:1, 0:16], d["ifg_row"])
            wt("s_dma", "tensor")
            wt("s_pool", "tensor")
            if on("tensor"):
                tick("s_pe", eng.matmul(bank(4, 16), ones_f[:, :], rowstage[0:1, 0:16], start=True, stop=True), 1)
            else:
                tick("s_pe", None, 1)
            wt("s_pe", "scalar")
            if on("scalar"):
                tick("s_act", eng.copy(ifgbc[:, :], bank(4, 16)), 1)
            else:
                tick("s_act", None, 1)

            wt("s_dve", "tensor")
            wt("s_act", "tensor")
            if on("tensor"):
                tick("s_pe", eng.matmul(pm8[0:44, 4 * 512 + 16:4 * 512 + 32], searr[0:64, :], qbh_s[:, :],
                                        start=True, stop=True), 1)
            else:
                tick("s_pe", None, 1)
            wt("s_pe", "scalar")
            if on("scalar"):
                tick("s_act", eng.copy(qsb44[:, :], pm8[0:44, 4 * 512 + 16:4 * 512 + 32]), 1)
            else:
                tick("s_act", None, 1)
            wt("s_act", "sync")
            dma("sync", "s_dma", d["qsbD"], qsb44[:, :])
            dma("sync", "s_dma", rowstage[0:1, 0:704], d["qsbD"].rearrange("i h -> (i h)").unsqueeze(0))
            wt("s_dma", "scalar")
            if on("scalar"):
                tick("s_act", eng.copy(qsbrow[:, :], rowstage[0:1, 0:704]), 1)
            else:
                tick("s_act", None, 1)

            def ln_phase(src_d, dst_fm, with_l2, store_snaps=None, n_tiles=NT, sqscr=None,
                         dual_ring=False):
                if sqscr is None:
                    sqscr = SCR[:, 0:1024]
                snapA = {}; snapSq = {}; snapTr = {}; snapEv = {}; snapL2 = {}
                wt("s_pool", "scalar")
                for tt in range(n_tiles):
                    ring = "scalar" if (dual_ring and tt % 2 == 1) else "sync"
                    lsem = "s_dmw" if ring == "scalar" else "s_dma"
                    b = (tt % 2) * 12
                    xv = xt[:, (tt % 3) * C:(tt % 3 + 1) * C]
                    xns = xnt[:, (tt % 2) * C:(tt % 2 + 1) * C]
                    if tt >= 3:
                        wtv("s_dve", ring, snapA[tt - 3])
                        wtv("s_act", ring, snapSq[tt - 3])
                    if store_snaps is not None:
                        wtv("s_dma", ring, store_snaps[tt])
                    dma(ring, lsem, xv, src_d[tt * 128:(tt + 1) * 128, :])
                    ld = cnt[lsem]
                    wtv(lsem, "vector", ld)
                    if on("vector"):
                        tick("s_dve", eng.reduce_sum(stat[:, b:b + 1], xv, AX.X), 1)
                    else:
                        tick("s_dve", None, 1)
                    wtv(lsem, "scalar", ld)
                    if tt >= 2:
                        wtv("s_dve", "scalar", snapA[tt - 2])
                    if on("scalar"):
                        tick("s_act", eng.activation(sqscr, xv, AF.Square,
                                                     accum_out=stat[:, b + 1:b + 2]), 1)
                    else:
                        tick("s_act", None, 1)
                    snapSq[tt] = cnt["s_act"]
                    wt("s_act", "vector")
                    vstep(lambda e: e.tensor_scalar(stat[:, b + 2:b + 3], stat[:, b:b + 1], 1.0 / C, None, ALU.mult))
                    vstep(lambda e: e.tensor_tensor(stat[:, b + 3:b + 4], stat[:, b + 2:b + 3], stat[:, b + 2:b + 3], ALU.mult))
                    vstep(lambda e: e.scalar_tensor_tensor(stat[:, b + 4:b + 5], stat[:, b + 1:b + 2], 1.0 / C,
                                                           stat[:, b + 3:b + 4], ALU.mult, ALU.subtract))
                    wt("s_dve", "scalar")
                    if on("scalar"):
                        tick("s_act", eng.activation(stat[:, b + 5:b + 6], stat[:, b + 4:b + 5], AF.Sqrt,
                                                     bias=np.float32(1e-5)), 1)
                    else:
                        tick("s_act", None, 1)
                    wt("s_act", "vector")
                    vstep(lambda e: e.reciprocal(stat[:, b + 6:b + 7], stat[:, b + 5:b + 6]))
                    if tt >= 2:
                        wtv("s_pe", "vector", snapTr[tt - 2])
                        if with_l2:
                            wtv("s_act", "vector", snapL2[tt - 2])
                    vstep(lambda e: e.tensor_scalar(xns, xv, stat[:, b + 2:b + 3], stat[:, b + 6:b + 7],
                                                    ALU.subtract, ALU.mult))
                    snapA[tt] = cnt["s_dve"]
                    if with_l2:
                        wt("s_dve", "scalar")
                        if on("scalar"):
                            tick("s_act", eng.activation(SCR[:, 0:1024], xns, AF.Square,
                                                         accum_out=stat[:, b + 7:b + 8]), 1)
                        else:
                            tick("s_act", None, 1)
                        snapL2[tt] = cnt["s_act"]
                        wt("s_act", "scalar")
                        if on("scalar"):
                            tick("s_act", eng.activation(stat[:, b + 8:b + 9], stat[:, b + 7:b + 8], AF.Sqrt,
                                                         bias=np.float32(1e-12)), 1)
                        else:
                            tick("s_act", None, 1)
                        wt("s_act", "vector")
                        vstep(lambda e, t=tt: e.reciprocal(rn_all[:, t:t + 1], stat[:, b + 8:b + 9]))
                    evc = {}
                    for c in range(CH):
                        if c == 0:
                            wt("s_dve", "tensor")
                        if c >= 2:
                            wtv("s_act", "tensor", evc[c - 2])
                        elif tt >= 1:
                            wtv("s_act", "tensor", snapEv[tt - 1])
                        else:
                            wt("s_act", "tensor")
                        if on("tensor"):
                            tick("s_pe", eng.transpose(pt_(c),
                                                       xns[:, c * 128:(c + 1) * 128], idn[:, :]), 1)
                        else:
                            tick("s_pe", None, 1)
                        wt("s_pe", "scalar")
                        if on("scalar"):
                            tick("s_act", eng.copy(dst_fm(c)[:, tt * 128:(tt + 1) * 128],
                                                   pt_(c)), 1)
                        else:
                            tick("s_act", None, 1)
                        evc[c] = cnt["s_act"]
                    snapTr[tt] = cnt["s_pe"]
                    snapEv[tt] = cnt["s_act"]

            ckpt("P0")
            import os as _os
            snap_p0_act = cnt["s_act"]
            def cv32(sl):
                return SCR[:, sl * 2048:(sl + 1) * 2048].bitcast(F32)

            def cvbf(sl):
                return BIG[:, 16384 + sl * 1024: 16384 + (sl + 1) * 1024]

            wtv("s_dve", "tensor", snap_searr_cp)
            ln_phase(d["x"], xnfm, False, n_tiles=int(_os.environ.get("LN_NTILES", NT)),
                     sqscr=qk_all[:, 0:1024])
            snap_p1_dma = cnt["s_dma"]
            snap_ln1_act = cnt["s_act"]

            ckpt("P1")
            def qwb():
                return SCR[:, 13824: 14848]

            def pbstage():
                return SCR[:, 14848: 15872].bitcast(F32)

            def qwb8(c_):
                return SCR[:, 7680 + c_ * 1024: 7680 + (c_ + 1) * 1024] if c_ < 6 else \
                    SCR[:, 13824 + (c_ - 6) * 1024: 13824 + (c_ - 5) * 1024]

            snap_wse_mm = {}
            snap_wse_evac = {}
            wt("s_pool", "scalar")
            wt("s_pool", "gpsimd")
            for c in range(CH):
                dma("gpsimd", "s_gp", qwb8(c), d["qw_q"][c * 128:(c + 1) * 128, :])
            snap_qwb = cnt["s_gp"]
            wtv("s_gp", "tensor", snap_qwb)
            kw = 0
            for c in range(CH):
                for cs in range(CH):
                    b = 4 + (kw % 2)
                    if kw >= 2:
                        wtv("s_act", "tensor", snap_wse_evac[kw - 2])
                    else:
                        wtv("s_act", "tensor", snap_p0_act)
                    if on("tensor"):
                        tick("s_pe", eng.matmul(bank(b, 88),
                                                qwb8(c)[:, cs * 128:(cs + 1) * 128],
                                                searr2[:, :], start=True, stop=True), 1)
                    else:
                        tick("s_pe", None, 1)
                    snap_wse_mm[kw] = cnt["s_pe"]
                    wtv("s_pe", "scalar", snap_wse_mm[kw])
                    if on("scalar"):
                        wv = wse(cs).rearrange("p (i h) -> p i h", h=16)
                        eng.copy(wv[:, :, 2 * c], bank(b, 88)[:, 0:44])
                        tick("s_act", eng.copy(wv[:, :, 2 * c + 1], bank(b, 88)[:, 44:88]), 1)
                    else:
                        tick("s_act", None, 1)
                    snap_wse_evac[kw] = cnt["s_act"]
                    kw += 1
            snap_wse_pe = cnt["s_pe"]

            ckpt("WSE")

            ckpt("PBD")
            wtv("s_dma", "gpsimd", snap_p1_dma)
            for wnm, wdst in (("fq", "wQbfD"), ("fk", "wKbfD"), ("fv", "wVbfD")):
                for e in range(E):
                    dma("gpsimd", "s_gp", d[wdst][e], d[wnm][e])
            for e in range(E):
                dma("gpsimd", "s_gp", d["oEbfD"][e], d["fo"][e])
            snap_wffn = cnt["s_gp"]
            ckpt("FFN")
            dma("gpsimd", "s_gp", d["outwD"], d["out_wT"])
            snap_outw_cast = cnt["s_gp"]
            if debug:
                wt("s_act", "sync")
                for cs_ in range(CH):
                    dma("sync", "s_dma", d["dbg_wse"][:, cs_ * 704:(cs_ + 1) * 704], wse(cs_))
            ckpt("P0W")
            def p2af32(sl_):
                return (SCR[:, 7680:11776] if sl_ == 0 else SCR[:, 11776:15872]).bitcast(F32)

            wt("s_pool", "sync")
            snap_p2a_mm = {}
            snap_p2a_evac = {}
            snap_p2a_cast = {}
            for jg in range(8):
                js0 = jg * 2
                wf32 = p2af32(jg % 2)
                if jg >= 2:
                    wtv("s_dve", "sync", snap_p2a_cast[jg - 2])
                if jg >= 1:
                    wtv("s_pe", "sync", snap_p2a_mm[jg - 1])
                else:
                    wtv("s_pe", "sync", snap_wse_pe)
                for c in range(CH):
                    dma("sync", "s_dma", wf32[:, c * 256:(c + 1) * 256],
                        d["qkv_wT"][c * 128:(c + 1) * 128, js0 * 128: js0 * 128 + 256])
                wtv("s_dma", "vector", cnt["s_dma"])
                if jg >= 1:
                    wtv("s_pe", "vector", snap_p2a_mm[jg - 1])
                if on("vector"):
                    last = None
                    for c in range(CH):
                        last = eng.tensor_copy(wcol(c), wf32[:, c * 256:(c + 1) * 256])
                    tick("s_dve", last, 1)
                else:
                    tick("s_dve", None, 1)
                snap_p2a_cast[jg] = cnt["s_dve"]
                for jsl in range(2):
                    js = js0 + jsl
                    for c in range(CH):
                        if c == 0:
                            wtv("s_dve", "tensor", snap_p2a_cast[jg])
                            wtv("s_act", "tensor", snap_ln1_act)
                            if (jg, jsl) > (0, 0):
                                pj = (jg, jsl - 1) if jsl == 1 else (jg - 1, 1)
                                wtv("s_act", "tensor", snap_p2a_evac[pj])
                        for tg in range(4):
                            if on("tensor"):
                                tick("s_pe", eng.matmul(bank(tg),
                                                        wcol(c)[:, jsl * 128:(jsl + 1) * 128],
                                                        xnfm(c)[:, tg * 512:(tg + 1) * 512],
                                                        start=(c == 0), stop=(c == CH - 1)), 1)
                            else:
                                tick("s_pe", None, 1)
                    snap_p2a_mm[jg] = cnt["s_pe"]
                    for tg in range(4):
                        wt("s_pe", "scalar")
                        dstc = Qfm(js) if js < 8 else Kfm(js - 8)
                        if on("scalar"):
                            tick("s_act", eng.activation(dstc[:, tg * 512:(tg + 1) * 512], bank(tg),
                                                         AF.Identity, bias=qbfm_s[:, js:js + 1], scale=np.float32(1.0)), 1)
                        else:
                            tick("s_act", None, 1)
                    snap_p2a_evac[(jg, jsl)] = cnt["s_act"]

            ckpt("P2a")
            wt("s_pool", "tensor")
            wt("s_pool", "sync")
            for zz in range(12):
                dma("sync", "s_dma", d["vD"][zz * 128:(zz + 1) * 128, 0:512], SCR[:, 15872:16384])
                dma("sync", "s_dma", d["vD"][zz * 128:(zz + 1) * 128, 512:1024], SCR[:, 15872:16384])
            wt("s_pe", "sync")
            snap_wvg_cast = {}
            for c in range(CH):
                for half, srcap in ((0, d["qkv_wT"][c * 128:(c + 1) * 128, 2048:3072]),
                                    (1, d["gate_wT"][c * 128:(c + 1) * 128, :])):
                    k2_ = 2 * c + half
                    wf32 = (SCR[:, 11776:13824] if k2_ % 2 == 0 else SCR[:, 13824:15872]).bitcast(F32)
                    if k2_ < 2:
                        wtv("s_dve", "sync", snap_p2a_cast[7])
                    else:
                        wtv("s_dve", "sync", snap_wvg_cast[k2_ - 2])
                    dma("sync", "s_dma", wf32, srcap)
                    wtv("s_dma", "vector", cnt["s_dma"])
                    if on("vector"):
                        tick("s_dve", eng.tensor_copy(wvg(c)[:, half * 1024:(half + 1) * 1024], wf32), 1)
                    else:
                        tick("s_dve", None, 1)
                    snap_wvg_cast[k2_] = cnt["s_dve"]
            snap_wvg_dve = cnt["s_dve"]

            def vstage2(pi_):
                return SCR[:, 11776: 12800] if pi_ == 0 else SCR[:, 13824: 14848]

            def gstage2(pi_):
                return SCR[:, 12800: 13824] if pi_ == 0 else SCR[:, 14848: 15872]

            snap_p2b_st = {}
            for tt in range(NT):
                for c in range(CH):
                    if tt == 0:
                        wtv("s_dve", "tensor", snap_wvg_cast[2 * c + 1])
                    if c == 0:
                        wt("s_act", "tensor")
                    lhs = xnfm(c)[:, tt * 128:(tt + 1) * 128]
                    if on("tensor"):
                        tick("s_pe", eng.matmul(bank(0), lhs, wvg(c)[:, 0:512], start=(c == 0), stop=False), 1)
                        tick("s_pe", eng.matmul(bank(1), lhs, wvg(c)[:, 512:1024], start=(c == 0), stop=False), 1)
                        tick("s_pe", eng.matmul(bank(2), lhs, wvg(c)[:, 1024:1536], start=(c == 0), stop=False), 1)
                        tick("s_pe", eng.matmul(bank(3), lhs, wvg(c)[:, 1536:2048], start=(c == 0), stop=False), 1)
                        tick("s_pe", eng.matmul(bank(4), lhs, wse(c)[:, 0:512], start=(c == 0), stop=False), 1)
                        tick("s_pe", eng.matmul(bank(5, 192), lhs, wse(c)[:, 512:704], start=(c == 0), stop=False), 1)
                    else:
                        tick("s_pe", None, 6)
                if on("tensor"):
                    tick("s_pe", eng.matmul(bank(0), ones_b[:, :], vbrow[:, 0:512], start=False, stop=True), 1)
                    tick("s_pe", eng.matmul(bank(1), ones_b[:, :], vbrow[:, 512:1024], start=False, stop=True), 1)
                    tick("s_pe", eng.matmul(bank(2), ones_b[:, :], gbrow[:, 0:512], start=False, stop=True), 1)
                    tick("s_pe", eng.matmul(bank(3), ones_b[:, :], gbrow[:, 512:1024], start=False, stop=True), 1)
                    tick("s_pe", eng.matmul(bank(4), ones_b[:, :], qsbrow[:, 0:512], start=False, stop=True), 1)
                    tick("s_pe", eng.matmul(bank(5, 192), ones_b[:, :], qsbrow[:, 512:704], start=False, stop=True), 1)
                else:
                    tick("s_pe", None, 6)
                wt("s_pe", "scalar")
                if tt >= 2:
                    wtv("s_dma", "scalar", snap_p2b_st[tt - 2])
                if on("scalar"):
                    eng.copy(vstage2(tt % 2)[:, 0:512], bank(0))
                    eng.copy(vstage2(tt % 2)[:, 512:1024], bank(1))
                    eng.activation(gstage2(tt % 2)[:, 0:512], bank(2), AF.Sigmoid)
                    tick("s_act", eng.activation(gstage2(tt % 2)[:, 512:1024], bank(3), AF.Sigmoid), 1)
                    t1ap = t1_all[:, :].rearrange("p (o s) -> p o s", s=256)[:, :, tt * 16: tt * 16 + 16]
                    eng.activation(t1ap[:, 0:32, :],
                                   bank(4).rearrange("p (o h) -> p o h", h=16),
                                   AF.Identity, bias=np.float32(INV), scale=np.float32(INV))
                    tick("s_act", eng.activation(t1ap[:, 32:44, :],
                                                 bank(5, 192).rearrange("p (o h) -> p o h", h=16),
                                                 AF.Identity, bias=np.float32(INV), scale=np.float32(INV)), 1)
                else:
                    tick("s_act", None, 2)
                wt("s_act", "sync")
                dma("sync", "s_dma", d["vD"][1536 + tt * 128:1536 + (tt + 1) * 128, :], vstage2(tt % 2))
                dma("sync", "s_dma", d["gateD"][tt * 128:(tt + 1) * 128, :], gstage2(tt % 2))
                snap_p2b_st[tt] = cnt["s_dma"]

            if debug:
                wt("s_dve", "sync")
                for c_ in range(CH):
                    dma("sync", "s_dma", d["dbg_wvg"][:, c_ * 2048:(c_ + 1) * 2048], wvg(c_))
            ckpt("P2b")
            if debug:
                wt("s_act", "sync")
                wt("s_dma", "sync")
                dma("sync", "s_dma", d["dbg_t1"], t1_all[:, :])
                dma("sync", "s_dma", d["dbg_v"], d["vD"][1536:, :])
                dma("sync", "s_dma", d["dbg_gate"], d["gateD"])
                dma("sync", "s_dma", d["dbg_qfm"][:, 0:2048], Qfm(0))
                dma("sync", "s_dma", d["dbg_qfm"][:, 2048:4096], Kfm(0))
            for c in range(CH):
                wt("s_act", "gpsimd")
                wt("s_dma", "gpsimd")
                if on("gpsimd"):
                    tick("s_pool", eng.tensor_copy(Kodd(c)[:, 0:2047], Kfm(c)[:, 1:2048]), 1)
                else:
                    tick("s_pool", None, 1)

            wt("s_act", "vector")
            wt("s_dma", "vector")
            if on("vector"):
                eng.memset(bank(4, 512), 0.0)
                tick("s_dve", eng.memset(bank(5, 512), 0.0), 1)
            else:
                tick("s_dve", None, 1)
            snap_pe_half = {}
            snap_dve_half = {}
            snap_act_evac = {}
            def tmph(h_, c_):
                return SCR[:, (h_ * 8 + c_) * 1024: (h_ * 8 + c_ + 1) * 1024]

            for oi, o in enumerate(OFF):
                even = (o % 2 == 0)
                pb = 4 + (oi % 2)
                for hf in range(2):
                    glo = hf * 1024
                    ghi = glo + 1024
                    lo = max(o, glo)
                    if lo >= ghi:
                        snap_dve_half[(oi, hf)] = (cnt["s_dve"], cnt["s_pool"])
                        snap_pe_half[(oi, hf)] = snap_pe_half.get((oi, hf - 1), snap_pe_half.get((oi - 1, 1), 0))
                        continue
                    for c in range(CH):
                        meng = "vector"
                        msem = "s_dve"
                        if oi == 0 and hf == 0:
                            if c == 0:
                                wt("s_pe", "vector")
                                wt("s_pool", "vector")
                        else:
                            prev = (oi - 1, hf) if (oi - 1, hf) in snap_pe_half else None
                            if prev is not None and c == 0:
                                wtv("s_pe", meng, snap_pe_half[prev])
                        if on(meng):
                            if o == 0 or even:
                                inst = eng.tensor_tensor(tmph(hf, c)[:, lo - glo:1024],
                                                         Qfm(c)[:, lo:ghi],
                                                         Kfm(c)[:, lo - o:ghi - o], ALU.mult)
                            else:
                                lo2 = max(o + 1, glo)
                                inst = eng.tensor_tensor(tmph(hf, c)[:, lo2 - glo:1024],
                                                         Qfm(c)[:, lo2:ghi],
                                                         Kodd(c)[:, lo2 - o - 1:ghi - o - 1], ALU.mult)
                                if lo2 == o + 1:
                                    inst = eng.tensor_tensor(tmph(hf, c)[:, o - glo:o - glo + 1],
                                                             Qfm(c)[:, o:o + 1],
                                                             Kfm(c)[:, 0:1], ALU.mult)
                            tick(msem, inst, 1)
                        else:
                            tick(msem, None, 1)
                    snap_dve_half[(oi, hf)] = (cnt["s_dve"], cnt["s_pool"])
                    if oi >= 2:
                        wtv("s_act", "tensor", snap_act_evac[oi - 2])
                    elif oi == 0 and hf == 0:
                        wt("s_act", "tensor")
                    wtv("s_dve", "tensor", snap_dve_half[(oi, hf)][0])
                    wtv("s_pool", "tensor", snap_dve_half[(oi, hf)][1])
                    for tt in range(max(tt_min(o), hf * 8), (hf + 1) * 8):
                        for c in range(CH):
                            if on("tensor"):
                                tick("s_pe", eng.matmul(pm8[:, pb * 512 + tt * 16: pb * 512 + tt * 16 + 16],
                                                        tmph(hf, c)[:, (tt - hf * 8) * 128:(tt - hf * 8 + 1) * 128],
                                                        h2[:, c * 16:(c + 1) * 16],
                                                        start=(c == 0), stop=(c == CH - 1)), 1)
                            else:
                                tick("s_pe", None, 1)
                    snap_pe_half[(oi, hf)] = cnt["s_pe"]
                wt("s_pe", "scalar")
                if on("scalar"):
                    tick("s_act", eng.copy(qk_all[:, oi * 256:(oi + 1) * 256], bank(pb, 256)), 1)
                else:
                    tick("s_act", None, 1)
                snap_act_evac[oi] = cnt["s_act"]

            if debug:
                wt("s_act", "sync")
                dma("sync", "s_dma", d["dbg_qk"], qk_all[:, :])
            wt("s_act", "vector")
            if debug:
                wt("s_dma", "vector")
            if on("vector"):
                tick("s_dve", eng.tensor_tensor(lgf32(), qk_all[:, :], t1_all[:, :], ALU.mult), 1)
            else:
                tick("s_dve", None, 1)
            for s in range(8):
                wt("s_dve", "sync")
                bcap_ = bass_rust.AP(d["pb_row"].tensor, s * 1408,
                                     bass_rust.VecI64Pair([[0, 128], [1, 1408]]))
                dma("sync", "s_dma", pbbuf(s % 2), bcap_)
                wt("s_dma", "vector")
                if on("vector"):
                    tick("s_dve", eng.tensor_tensor(lgf32()[:, s * 1408:(s + 1) * 1408],
                                                    lgf32()[:, s * 1408:(s + 1) * 1408],
                                                    pbbuf(s % 2), ALU.add), 1)
                else:
                    tick("s_dve", None, 1)
            wt("s_dve", "vector")
            if on("vector"):
                tick("s_dve", eng.tensor_scalar(lgf32(), lgf32(), 30.0, None, ALU.min), 1)
            else:
                tick("s_dve", None, 1)
            wt("s_dve", "scalar")
            if on("scalar"):
                tick("s_act", eng.activation(t1_all[:, :], lgf32(), AF.Exp), 1)
            else:
                tick("s_act", None, 1)
            for s in range(4):
                wt("s_dve", "sync")
                dma("sync", "s_dma", spbuf(s % 2), d["SPMASK"][:, s * 2816:(s + 1) * 2816])
                wt("s_dma", "vector")
                wt("s_act", "vector")
                if on("vector"):
                    tick("s_dve", eng.tensor_tensor(qk_all[:, s * 2816:(s + 1) * 2816],
                                                    t1_all[:, s * 2816:(s + 1) * 2816],
                                                    spbuf(s % 2), ALU.mult), 1)
                else:
                    tick("s_dve", None, 1)
            for tt in range(NT):
                srcz = qk_all[:, :].rearrange("p (o s) -> p o s", s=256)[:, :, tt * 16: tt * 16 + 16]
                srcr = srcz.transpose([0, 2, 1])
                vstep(lambda e, s=srcr, t=tt: e.tensor_reduce(
                    z_all[:, t * 16:(t + 1) * 16].unsqueeze(2), s, AX.X, ALU.add))
                vstep(lambda e, t=tt: e.reciprocal(
                    zr_all[:, t * 16:(t + 1) * 16], z_all[:, t * 16:(t + 1) * 16]))
                vstep(lambda e, t=tt: e.tensor_tensor(
                    w2_all[:, t * 16:(t + 1) * 16], zr_all[:, t * 16:(t + 1) * 16],
                    ifgbc[:, :], ALU.mult))
            for tt in range(NT):
                src = qk_all[:, :].rearrange("p (o s) -> p o s", s=256)[:, :, tt * 16: tt * 16 + 16]
                dst = t1_all[:, :].rearrange("p (o s) -> p o s", s=256)[:, :, tt * 16: tt * 16 + 16]
                w2v = w2_all[:, tt * 16:(tt + 1) * 16].unsqueeze(1).broadcast_to([128, NO, 16])
                wt("s_dve", "vector")
                if on("vector"):
                    tick("s_dve", eng.tensor_tensor(dst, src, w2v, ALU.mult), 1)
                else:
                    tick("s_dve", None, 1)

            ckpt("P4")
            if debug:
                wt("s_dve", "sync")
                wt("s_act", "sync")
                dma("sync", "s_dma", d["dbg_lg"], lgf32())
                dma("sync", "s_dma", d["dbg_pm"], qk_all[:, :])
                dma("sync", "s_dma", d["dbg_pf"], t1_all[:, :])
            wt("s_dve", "gpsimd")
            if on("gpsimd"):
                for pi_ in range(6):
                    last = eng.memset(vsbuf(pi_), 0.0)
                tick("s_pool", last, 1)
            else:
                tick("s_pool", None, 1)
            wt("s_pool", "sync")
            wt("s_pool", "vector")
            wt("s_pool", "gpsimd")
            wt("s_dma", "scalar")
            wt("s_dve", "scalar")
            wt("s_act", "scalar")
            for tt in range(NT):
                dma("scalar", "s_dmw", Vres(tt), d["vD"][1536 + tt * 128:1536 + (tt + 1) * 128, :])
            snap_vres = cnt["s_dmw"]
            wtv("s_gp", "scalar", snap_outw_cast)
            for c in range(CH):
                dma("scalar", "s_dmw", outwT(c), d["outwD"][c * 128:(c + 1) * 128, :])
            snap_outw = cnt["s_dmw"]
            wtv("s_dmw", "sync", snap_vres)
            wtv("s_dmw", "vector", snap_vres)
            wtv("s_dmw", "gpsimd", snap_vres)
            PVD = 6
            kglob = [0]
            snap_mult = []
            snap_pvmm = []
            snap_ofevac = {}
            snap_resid = {}
            snap_x2st = {}
            snap_og = {}
            snap_tr = {}
            snap_ogfm = {}
            for tt in range(NT):
                t0 = tt * 128
                valid = [oi for oi, o in enumerate(OFF) if t0 + 127 >= o]
                for k, oi in enumerate(valid):
                    o = OFF[oi]
                    a = t0 - o
                    kg = kglob[0]
                    pi = kg % PVD
                    aligned = (a >= 0 and a % 128 == 0)
                    muleng = "gpsimd" if kg % 9 in (1, 3, 5, 7) else "vector"
                    ldw = None
                    if not aligned:
                        ring, lsem = ("sync", "s_dvs") if kg % 2 == 0 else ("scalar", "s_dv2")
                        if kg >= PVD:
                            s_, v_ = snap_mult[kg - PVD]
                            wtv(s_, ring, v_)
                        dma(ring, lsem, vsbuf(pi), d["vD"][1536 + a:1536 + a + 128, :])
                        ldw = (lsem, cnt[lsem])
                    pcol = t1_all[:, oi * 256 + tt * 16: oi * 256 + tt * 16 + 16]
                    pbv = pcol.unsqueeze(2).broadcast_to([128, 16, 64])
                    if ldw is not None:
                        wtv(ldw[0], muleng, ldw[1])
                    if kg >= PVD:
                        wtv("s_pe", muleng, snap_pvmm[kg - PVD])
                    vsrc = Vres(a // 128) if aligned else vsbuf(pi)
                    if on(muleng):
                        tick("s_pool" if muleng == "gpsimd" else "s_dve",
                             eng.tensor_tensor(prodbuf(pi).rearrange("p (h e) -> p h e", e=64),
                                               vsrc.rearrange("p (h e) -> p h e", e=64),
                                               pbv, ALU.mult), 1)
                    else:
                        tick("s_pool" if muleng == "gpsimd" else "s_dve", None, 1)
                    msem = "s_pool" if muleng == "gpsimd" else "s_dve"
                    snap_mult.append((msem, cnt[msem]))
                    wtv(msem, "tensor", cnt[msem])
                    if k == 0 and tt >= 1:
                        wtv("s_act", "tensor", snap_ofevac[tt - 1])
                    if on("tensor"):
                        tick("s_pe", eng.matmul(bank(0), idn[:, :], prodbuf(pi)[:, 0:512],
                                                start=(k == 0), stop=(k == len(valid) - 1)), 1)
                        tick("s_pe", eng.matmul(bank(1), idn[:, :], prodbuf(pi)[:, 512:1024],
                                                start=(k == 0), stop=(k == len(valid) - 1)), 1)
                    else:
                        tick("s_pe", None, 2)
                    snap_pvmm.append(cnt["s_pe"])
                    kglob[0] += 1
                wt("s_pe", "scalar")
                if tt >= 1:
                    wtv("s_dve", "scalar", snap_og[tt - 1])
                if on("scalar"):
                    eng.copy(ofbuf()[:, 0:512], bank(0))
                    tick("s_act", eng.copy(ofbuf()[:, 512:1024], bank(1)), 1)
                else:
                    tick("s_act", None, 1)
                snap_ofevac[tt] = cnt["s_act"]
                if debug:
                    wt("s_act", "sync")
                    dma("sync", "s_dma", d["dbg_of"][t0:t0 + 128, :], ofbuf())
                if tt >= 1:
                    wtv("s_dve", "sync", snap_og[tt - 1])
                dma("sync", "s_dma", gtbuf(), d["gateD"][t0:t0 + 128, :])
                wt("s_dma", "vector")
                wt("s_act", "vector")
                if tt >= 1:
                    wtv("s_pe", "vector", snap_tr[tt - 1])
                if on("vector"):
                    tick("s_dve", eng.tensor_tensor(ogbuf(), ofbuf(), gtbuf(), ALU.mult), 1)
                else:
                    tick("s_dve", None, 1)
                snap_og[tt] = cnt["s_dve"]
                evc5 = {}
                for c in range(CH):
                    if c == 0:
                        wt("s_dve", "tensor")
                    if c >= 2:
                        wtv("s_act", "tensor", evc5[c - 2])
                    elif tt >= 1:
                        wtv("s_act", "tensor", snap_ogfm[tt - 1])
                    else:
                        wt("s_act", "tensor")
                    if on("tensor"):
                        tick("s_pe", eng.transpose(pt_(c),
                                                   ogbuf()[:, c * 128:(c + 1) * 128], idn[:, :]), 1)
                    else:
                        tick("s_pe", None, 1)
                    wt("s_pe", "scalar")
                    if on("scalar"):
                        tick("s_act", eng.copy(ogfm()[:, c * 128:(c + 1) * 128],
                                               pt_(c)), 1)
                    else:
                        tick("s_act", None, 1)
                    evc5[c] = cnt["s_act"]
                snap_tr[tt] = cnt["s_pe"]
                snap_ogfm[tt] = cnt["s_act"]
                for c in range(CH):
                    if c == 0:
                        wt("s_act", "tensor")
                        wtv("s_dmw", "tensor", snap_outw)
                        if tt >= 1:
                            wtv("s_dve", "tensor", snap_resid[tt - 1])
                    if on("tensor"):
                        tick("s_pe", eng.matmul(bank(2), ogfm()[:, c * 128:(c + 1) * 128],
                                                outwT(c)[:, 0:512], start=(c == 0), stop=False), 1)
                        tick("s_pe", eng.matmul(bank(3), ogfm()[:, c * 128:(c + 1) * 128],
                                                outwT(c)[:, 512:1024], start=(c == 0), stop=False), 1)
                    else:
                        tick("s_pe", None, 2)
                if on("tensor"):
                    tick("s_pe", eng.matmul(bank(2), ones_b[:, :], obrow[:, 0:512], start=False, stop=True), 1)
                    tick("s_pe", eng.matmul(bank(3), ones_b[:, :], obrow[:, 512:1024], start=False, stop=True), 1)
                else:
                    tick("s_pe", None, 2)
                xv = xt[:, (tt % 2) * C:(tt % 2) * C + C]
                x2v = xt[:, ((tt + 1) % 2) * C:((tt + 1) % 2) * C + C]
                if tt >= 1:
                    wtv("s_dve", "sync", snap_resid[tt - 1])
                    wtv("s_dma", "sync", snap_x2st[tt - 1])
                dma("sync", "s_dma", xv, d["x"][t0:t0 + 128, :])
                snap_xload = cnt["s_dma"]
                wt("s_pe", "vector")
                wtv("s_dma", "vector", snap_xload)
                if on("vector"):
                    eng.tensor_tensor(x2v[:, 0:512], bank(2), xv[:, 0:512], ALU.add)
                    tick("s_dve", eng.tensor_tensor(x2v[:, 512:1024], bank(3), xv[:, 512:1024], ALU.add), 1)
                else:
                    tick("s_dve", None, 1)
                snap_resid[tt] = cnt["s_dve"]
                wtv("s_dve", "sync", snap_resid[tt])
                dma("sync", "s_dma", d["x2D"][t0:t0 + 128, :], x2v)
                snap_x2st[tt] = cnt["s_dma"]

            wt("s_dve", "gpsimd")
            wt("s_pool", "gpsimd")
            wt("s_pe", "gpsimd")
            for e in range(E):
                dst = qk_all[:, e * 1024:(e + 1) * 1024] if e < 11 else t1_all[:, (e - 11) * 1024:(e - 10) * 1024]
                dma("gpsimd", "s_gp", dst, d["oEbfD"][e])
            snap_oeall = cnt["s_gp"]
            ckpt("P5")
            if debug:
                wt("s_dma", "sync")
                dma("sync", "s_dma", d["dbg_x2"], d["x2D"])
                wt("s_dma", "tensor")
                wt("s_dma", "scalar")
            wt("s_dve", "scalar")
            wt("s_pool", "scalar")
            ln_phase(d["x2D"], xn2fm, True, store_snaps=snap_x2st)

            ckpt("P6")
            wt("s_pool", "vector")
            wt("s_pool", "scalar")
            if on("vector"):
                tick("s_dve", eng.memset(stat[:, 9:10], 1.0), 1)
            else:
                tick("s_dve", None, 1)
            wt("s_dve", "tensor")
            wt("s_act", "tensor")
            for c in range(CH):
                if on("tensor"):
                    tick("s_pe", eng.matmul(pm8[0:E, 4 * 512:4 * 512 + 1], sim2[:, c * E:(c + 1) * E],
                                            stat[:, 9:10], start=(c == 0), stop=(c == CH - 1)), 1)
                else:
                    tick("s_pe", None, 1)
            wt("s_pe", "scalar")
            if on("scalar"):
                tick("s_act", eng.activation(stat[0:E, 10:11], pm8[0:E, 4 * 512:4 * 512 + 1],
                                             AF.Sqrt, bias=np.float32(1e-12)), 1)
            else:
                tick("s_act", None, 1)
            wt("s_act", "vector")
            if on("vector"):
                tick("s_dve", eng.reciprocal(stat[0:E, 11:12], stat[0:E, 10:11]), 1)
            else:
                tick("s_dve", None, 1)
            wt("s_dve", "sync")
            dma("sync", "s_dma", d["snD"].rearrange("r e -> (r e)").unsqueeze(1), stat[0:E, 11:12])
            dma("sync", "s_dma", snrow[:, :], d["snD"])
            dma("sync", "s_dma", rowstage[0:1, 0:16], d["gates_row"])
            wt("s_dma", "scalar")
            if on("scalar"):
                tick("s_act", eng.activation(sgrow[:, :], rowstage[0:1, 0:16], AF.Sigmoid), 1)
            else:
                tick("s_act", None, 1)
            wt("s_act", "tensor")
            wt("s_dma", "tensor")
            if on("tensor"):
                tick("s_pe", eng.matmul(pm8[:, 4 * 512:4 * 512 + E], ones_f[:, :], snrow[:, :],
                                        start=True, stop=True), 1)
            else:
                tick("s_pe", None, 1)
            wt("s_pe", "scalar")
            if on("scalar"):
                tick("s_act", eng.copy(snbc[:, :], pm8[:, 4 * 512:4 * 512 + E]), 1)
            else:
                tick("s_act", None, 1)
            wt("s_act", "tensor")
            if on("tensor"):
                tick("s_pe", eng.matmul(pm8[:, 4 * 512:4 * 512 + E], ones_f[:, :], sgrow[:, :],
                                        start=True, stop=True), 1)
            else:
                tick("s_pe", None, 1)
            wt("s_pe", "scalar")
            if on("scalar"):
                tick("s_act", eng.copy(sgbc[:, :], pm8[:, 4 * 512:4 * 512 + E]), 1)
            else:
                tick("s_act", None, 1)

            for tt in range(NT):
                for c in range(CH):
                    wt("s_act", "tensor")
                    if on("tensor"):
                        tick("s_pe", eng.matmul(bank(5, E), xn2fm(c)[:, tt * 128:(tt + 1) * 128],
                                                simb[:, c * E:(c + 1) * E],
                                                start=(c == 0), stop=(c == CH - 1)), 1)
                    else:
                        tick("s_pe", None, 1)
                wt("s_pe", "scalar")
                if on("scalar"):
                    tick("s_act", eng.activation(p7f(0), bank(5, E), AF.Identity,
                                                 bias=np.float32(0.0), scale=rn_all[:, tt:tt + 1]), 1)
                else:
                    tick("s_act", None, 1)
                wt("s_act", "vector")
                lgg = p7f(0)
                vstep(lambda e: e.tensor_tensor(lgg, lgg, snbc[:, :], ALU.mult))
                vstep(lambda e: e.tensor_tensor(lgg, lgg, sgbc[:, :], ALU.subtract))
                gated = p7f(1)
                vstep(lambda e: e.tensor_scalar(gated, lgg, 0.0, None, ALU.max))
                maskp = p7f(2)
                vstep(lambda e: e.tensor_scalar(maskp, lgg, 0.0, None, ALU.is_gt))
                vstep(lambda e: e.reduce_sum(stat[:, 0:1], maskp, AX.X))
                vstep(lambda e: e.tensor_scalar(stat[:, 1:2], stat[:, 0:1], 0.5, None, ALU.is_lt))
                vstep(lambda e: e.tensor_reduce(stat[:, 2:3].unsqueeze(2), lgg.unsqueeze(1), AX.X, ALU.max))
                ge1 = p7f(3)
                vstep(lambda e: e.tensor_scalar(ge1, lgg, stat[:, 2:3], None, ALU.is_ge))
                lm = p7f(4)
                vstep(lambda e: e.scalar_tensor_tensor(lm, ge1, NEG, lgg, ALU.mult, ALU.add))
                vstep(lambda e: e.tensor_reduce(stat[:, 3:4].unsqueeze(2), lm.unsqueeze(1), AX.X, ALU.max))
                fbm = p7f(5)
                vstep(lambda e: e.tensor_scalar(fbm, lgg, stat[:, 3:4], None, ALU.is_ge))
                tm = p7f(6)
                vstep(lambda e: e.scalar_tensor_tensor(tm, fbm, stat[:, 1:2], maskp, ALU.mult, ALU.add))
                wt("s_dve", "scalar")
                if on("scalar"):
                    tick("s_act", eng.activation(p7f(7), p7f(1), AF.Exp), 1)
                else:
                    tick("s_act", None, 1)
                wt("s_act", "vector")
                ex = p7f(7)
                vstep(lambda e: e.tensor_tensor(ex, ex, p7f(6), ALU.mult))
                vstep(lambda e: e.reduce_sum(stat[:, 4:5], ex, AX.X))
                vstep(lambda e: e.reciprocal(stat[:, 5:6], stat[:, 4:5]))
                vstep(lambda e, t=tt: e.tensor_scalar(rw_all[:, t * E:(t + 1) * E], ex,
                                                      stat[:, 5:6], None, ALU.mult))

            ckpt("P7")
            snap_we = {}
            snap_diag = {}
            snap_rawevac = {}
            snap_dg = {}
            idx8 = [0]
            pend = [None]

            def emit_diag(ix, qi, e):
                wtv("s_dve", "tensor", snap_dg[ix])
                wtv("s_act", "tensor", snap_rawevac[ix])
                if on("tensor"):
                    tick("s_pe", eng.matmul(pm8[:, qi * 512: qi * 512 + 384],
                                            dgbuf(ix % 2), rawsb(ix % 2),
                                            start=(e == 0), stop=(e == E - 1)), 1)
                else:
                    tick("s_pe", None, 1)
                snap_diag[ix] = cnt["s_pe"]

            snap_weld = {}
            for g in range(4):
                for e in range(E):
                    pi = (g * E + e) % 3
                    eg = g * E + e
                    ring, lsem = ("sync", "s_dvs") if eg % 2 == 0 else ("scalar", "s_dv2")
                    if eg >= 3:
                        wtv("s_pe", ring, snap_we[eg - 3])
                    else:
                        wt("s_pe", ring)
                        wt("s_dve", ring)
                    wtv("s_gp", ring, snap_wffn)
                    for wi_, wsrc_ in enumerate(("wQbfD", "wKbfD", "wVbfD")):
                        dma(ring, lsem,
                            webuf(pi).rearrange("p (k w) -> p k w", k=CH)[:, :, wi_ * 128:(wi_ + 1) * 128],
                            d[wsrc_][e].rearrange("(k c) h -> c k h", c=128))
                    snap_weld[eg] = (lsem, cnt[lsem])
                    for qi in range(4):
                        qt = g * 4 + qi
                        ix = idx8[0]
                        rb = 4 + (ix % 2)
                        for c in range(CH):
                            if c == 0:
                                wtv(snap_weld[eg][0], "tensor", snap_weld[eg][1])
                                if ix >= 2:
                                    wtv("s_act", "tensor", snap_rawevac[ix - 2])
                                else:
                                    wt("s_act", "tensor")
                            if on("tensor"):
                                tick("s_pe", eng.matmul(pm8[:, rb * 512:rb * 512 + 384],
                                                        xn2fm(c)[:, qt * 128:(qt + 1) * 128],
                                                        webuf(pi)[:, c * 384:(c + 1) * 384],
                                                        start=(c == 0), stop=(c == CH - 1)), 1)
                            else:
                                tick("s_pe", None, 1)
                        snap_we[eg] = cnt["s_pe"]
                        wt("s_pe", "scalar")
                        if on("scalar"):
                            tick("s_act", eng.copy(rawsb(ix % 2), pm8[:, rb * 512:rb * 512 + 384]), 1)
                        else:
                            tick("s_act", None, 1)
                        snap_rawevac[ix] = cnt["s_act"]
                        if ix >= 2:
                            wtv("s_pe", "vector", snap_diag[ix - 2])
                        if on("vector"):
                            tick("s_dve", eng.tensor_scalar(dgbuf(ix % 2), idn[:, :],
                                                            rw_all[:, qt * E + e: qt * E + e + 1],
                                                            None, ALU.mult), 1)
                        else:
                            tick("s_dve", None, 1)
                        snap_dg[ix] = cnt["s_dve"]
                        if pend[0] is not None:
                            emit_diag(*pend[0])
                        pend[0] = (ix, qi, e)
                        idx8[0] += 1
                emit_diag(*pend[0])
                pend[0] = None
                for qi in range(4):
                    qt = g * 4 + qi
                    wt("s_pe", "scalar")
                    if on("scalar"):
                        eng.copy(qkscr()[:, 0:128], pm8[:, qi * 512: qi * 512 + 128])
                        eng.copy(qkscr()[:, 128:256], pm8[:, qi * 512 + 128: qi * 512 + 256])
                        tick("s_act", eng.copy(V2(qt), pm8[:, qi * 512 + 256: qi * 512 + 384]), 1)
                    else:
                        tick("s_act", None, 1)
                    for wi, dstv in ((0, q2fm), (1, k2fm)):
                        wt("s_act", "tensor")
                        if on("tensor"):
                            tick("s_pe", eng.transpose(pt_(wi),
                                                       qkscr()[:, wi * 128:(wi + 1) * 128], idn[:, :]), 1)
                        else:
                            tick("s_pe", None, 1)
                        wt("s_pe", "scalar")
                        if on("scalar"):
                            tick("s_act", eng.copy(dstv[:, qt * 128:(qt + 1) * 128],
                                                   pt_(wi)), 1)
                        else:
                            tick("s_act", None, 1)

            ckpt("P8")
            wt("s_pe", "sync")
            dma("sync", "s_dma", causal(), d["CM4"])
            FSC = float(1.0 / np.sqrt(128.0))
            for g in range(4):
                njt = 4 * g + 4
                for j in range(njt):
                    pb = 2 + (j % 2)
                    wt("s_act", "tensor")
                    if on("tensor"):
                        tick("s_pe", eng.matmul(pm8[:, pb * 512:(pb + 1) * 512],
                                                k2fm[:, j * 128:(j + 1) * 128],
                                                q2fm[:, g * 512:(g + 1) * 512],
                                                start=True, stop=True), 1)
                    else:
                        tick("s_pe", None, 1)
                    if j >= 4 * g:
                        kk = j - 4 * g
                        wt("s_pe", "vector")
                        wt("s_dma", "vector")
                        if on("vector"):
                            tick("s_dve", eng.tensor_tensor(pm8[:, pb * 512:(pb + 1) * 512],
                                                            pm8[:, pb * 512:(pb + 1) * 512],
                                                            causal()[:, kk * 512:(kk + 1) * 512],
                                                            ALU.add), 1)
                        else:
                            tick("s_dve", None, 1)
                    wt("s_pe", "scalar")
                    wt("s_dve", "scalar")
                    if on("scalar"):
                        tick("s_act", eng.activation(expS(j), pm8[:, pb * 512:(pb + 1) * 512],
                                                     AF.Exp, scale=np.float32(FSC)), 1)
                    else:
                        tick("s_act", None, 1)
                wt("s_act", "tensor")
                for j in range(njt):
                    if on("tensor"):
                        tick("s_pe", eng.matmul(pm8[0:1, 5 * 512:5 * 512 + 512], onecol_b[:, :],
                                                expS(j), start=(j == 0), stop=(j == njt - 1)), 1)
                    else:
                        tick("s_pe", None, 1)
                wt("s_pe", "scalar")
                if on("scalar"):
                    tick("s_act", eng.copy(zrow()[:, g * 512:(g + 1) * 512],
                                           pm8[0:1, 5 * 512:5 * 512 + 512]), 1)
                else:
                    tick("s_act", None, 1)
                for j in range(njt):
                    if on("tensor"):
                        tick("s_pe", eng.matmul(pm8[:, 4 * 512:4 * 512 + 512], V2(j),
                                                expS(j), start=(j == 0), stop=(j == njt - 1)), 1)
                    else:
                        tick("s_pe", None, 1)
                wt("s_pe", "scalar")
                if on("scalar"):
                    tick("s_act", eng.copy(aofm[:, g * 512:(g + 1) * 512],
                                           pm8[:, 4 * 512:4 * 512 + 512]), 1)
                else:
                    tick("s_act", None, 1)
            wt("s_act", "sync")
            dma("sync", "s_dma", d["zD"], zrow())
            dma("sync", "s_dma", zcol[:, :],
                d["zD"].rearrange("r (t p) -> (r p) t", p=128), allow_slow_non_contiguous=True)
            wt("s_dma", "vector")
            vstep(lambda e: e.reciprocal(zrcol[:, :], zcol[:, :]))
            for tt in range(NT):
                vstep(lambda e, t=tt: e.tensor_scalar(wz_all[:, t * E:(t + 1) * E],
                                                      rw_all[:, t * E:(t + 1) * E],
                                                      zrcol[:, t:t + 1], None, ALU.mult))
            for qt in range(NT):
                wt("s_act", "tensor")
                if on("tensor"):
                    tick("s_pe", eng.transpose(pt_(qt),
                                               aofm[:, qt * 128:(qt + 1) * 128], idn[:, :]), 1)
                else:
                    tick("s_pe", None, 1)
                wt("s_pe", "scalar")
                if on("scalar"):
                    tick("s_act", eng.copy(aotok(qt), pt_(qt)), 1)
                else:
                    tick("s_act", None, 1)

            ckpt("P9")
            ogroups = [[0, 1, 2], [3, 4, 5], [6, 7, 8], [9, 10, 11], [12, 13, 14], [15]]
            isc = [0]
            snap_scmm = {}
            snap_scevac = {}
            snap_scdg = {}
            for og in ogroups:
                for qi, qt in enumerate(og):
                    for e in range(E):
                        i = isc[0]
                        if i >= 4:
                            wtv("s_pe", "vector", snap_scmm[i - 4])
                        else:
                            wt("s_act", "vector")
                            wt("s_pe", "vector")
                        if on("vector"):
                            tick("s_dve", eng.tensor_scalar(dgbuf(i % 4), idn[:, :],
                                                            wz_all[:, qt * E + e: qt * E + e + 1],
                                                            None, ALU.mult), 1)
                        else:
                            tick("s_dve", None, 1)
                        snap_scdg[i] = cnt["s_dve"]
                        wtv("s_dve", "tensor", snap_scdg[i])
                        if i >= 2:
                            wtv("s_act", "tensor", snap_scevac[i - 2])
                        else:
                            wt("s_act", "tensor")
                        ptf = (pt1 if i % 2 == 0 else pt2).bitcast(F32)[:, 0:128]
                        if on("tensor"):
                            tick("s_pe", eng.matmul(ptf, aotok(qt),
                                                    dgbuf(i % 4), start=True, stop=True), 1)
                        else:
                            tick("s_pe", None, 1)
                        snap_scmm[i] = cnt["s_pe"]
                        wtv("s_pe", "scalar", snap_scmm[i])
                        if on("scalar"):
                            tick("s_act", eng.copy(scfm(qi, e), ptf), 1)
                        else:
                            tick("s_act", None, 1)
                        snap_scevac[i] = cnt["s_act"]
                        isc[0] += 1
                for e in range(E):
                    oev = qk_all[:, e * 1024:(e + 1) * 1024] if e < 11 else t1_all[:, (e - 11) * 1024:(e - 10) * 1024]
                    for qi, qt in enumerate(og):
                        if e == 0 and qi == 0:
                            wtv("s_gp", "tensor", snap_oeall)
                            wt("s_act", "tensor")
                        if on("tensor"):
                            tick("s_pe", eng.matmul(bank(2 * qi), scfm(qi, e),
                                                    oev[:, 0:512], start=(e == 0), stop=(e == E - 1)), 1)
                            tick("s_pe", eng.matmul(bank(2 * qi + 1), scfm(qi, e),
                                                    oev[:, 512:1024], start=(e == 0), stop=(e == E - 1)), 1)
                        else:
                            tick("s_pe", None, 2)
                for qi, qt in enumerate(og):
                    t0 = qt * 128
                    xv = xt[:, (qt % 2) * C:(qt % 2) * C + C]
                    x2v = xt[:, ((qt + 1) % 2) * C:((qt + 1) % 2) * C + C]
                    wt("s_dve", "sync")
                    wt("s_dma", "sync")
                    wt("s_st", "sync")
                    dma("sync", "s_dma", xv, d["x2D"][t0:t0 + 128, :])
                    wt("s_pe", "vector")
                    wt("s_dma", "vector")
                    if on("vector"):
                        eng.tensor_tensor(x2v[:, 0:512], bank(2 * qi), xv[:, 0:512], ALU.add)
                        tick("s_dve", eng.tensor_tensor(x2v[:, 512:1024], bank(2 * qi + 1),
                                                        xv[:, 512:1024], ALU.add), 1)
                    else:
                        tick("s_dve", None, 1)
                    wt("s_dve", "sync")
                    dma("sync", "s_st", d["y"][t0:t0 + 128, :], x2v)

            if debug:
                wt("s_dve", "sync")
                wt("s_act", "sync")
                dma("sync", "s_dma", d["dbg_rw"], rw_all[:, :])
                dma("sync", "s_dma", d["dbg_wz"], wz_all[:, :])
                dma("sync", "s_dma", d["dbg_zc"], zcol[:, :])
                dma("sync", "s_dma", d["dbg_rn"], rn_all[:, :])
                dma("sync", "s_dma", d["dbg_z"], z_all[:, :])
                dma("sync", "s_dma", d["dbg_zr"], zr_all[:, :])
                dma("sync", "s_dma", d["dbg_w2"], w2_all[:, :])
                dma("sync", "s_dma", d["dbg_ifg"], ifgbc[:, :])
                wt("s_dma", "sync")
        except _StopEmit:
            pass
        for s_ in semnames:
            eng.wait_ge(sems[s_], cnt[s_])

    with nc.Block() as blk:
        blk.sync(lambda e: walk("sync", e))
        blk.tensor(lambda e: walk("tensor", e))
        blk.vector(lambda e: walk("vector", e))
        blk.scalar(lambda e: walk("scalar", e))
        blk.gpsimd(lambda e: walk("gpsimd", e))

    return nc


def _host_inputs(xb, w):
    bf = ml_dtypes.bfloat16
    m = {}
    m["x"] = np.ascontiguousarray(xb, dtype=np.float32)
    qkv_w = np.asarray(w["L0_attn_qkv_w"], dtype=np.float32)
    m["qkv_wT"] = np.ascontiguousarray(qkv_w.T)
    m["qw_q"] = np.ascontiguousarray(qkv_w[0:C])
    m["gate_wT"] = np.ascontiguousarray(np.asarray(w["L0_attn_gate_w"], np.float32).T)
    m["out_wT"] = np.ascontiguousarray(np.asarray(w["L0_attn_out_w"], np.float32).T)
    qb = np.asarray(w["L0_attn_qkv_b"], np.float32)
    m["qb_fm"] = np.ascontiguousarray(qb[0:2048].reshape(16, 128).T)
    m["vb_row"] = np.ascontiguousarray(qb[2048:3072].reshape(1, C))
    m["qbh"] = np.ascontiguousarray(qb[0:1024].reshape(16, 64).T)
    m["gb_row"] = np.ascontiguousarray(np.asarray(w["L0_attn_gate_b"], np.float32).reshape(1, C))
    m["ob_row"] = np.ascontiguousarray(np.asarray(w["L0_attn_out_b"], np.float32).reshape(1, C))
    m["se"] = np.ascontiguousarray(np.asarray(w["L0_attn_scale_embed"], np.float32))
    pb = np.asarray(w["L0_attn_pos_bias"], np.float32)
    pbrow = np.empty((NO, 16, 16), np.float32)
    pbrow[:, :, :] = pb[:, None, :]
    m["pb_row"] = np.ascontiguousarray(pbrow.reshape(1, COLW))
    m["ifg_row"] = np.ascontiguousarray(np.asarray(w["L0_attn_if_gain"], np.float32).reshape(1, 16))
    m["sim"] = np.ascontiguousarray(np.asarray(w["L0_ffn_sim_matrix"], np.float32))
    m["gates_row"] = np.ascontiguousarray(np.asarray(w["L0_ffn_gates"], np.float32).reshape(1, E))
    m["fq"] = np.ascontiguousarray(np.asarray(w["L0_ffn_q_proj"], np.float32))
    m["fk"] = np.ascontiguousarray(np.asarray(w["L0_ffn_k_proj"], np.float32))
    m["fv"] = np.ascontiguousarray(np.asarray(w["L0_ffn_v_proj"], np.float32))
    m["fo"] = np.ascontiguousarray(np.asarray(w["L0_ffn_o_proj"], np.float32))
    m["IDN"] = np.eye(128, dtype=bf)
    r = np.arange(128)[:, None]
    sp = np.zeros((128, NO, 16, 16), bf)
    for oi, o in enumerate(OFF):
        for tt in range(16):
            sp[:, oi, tt, :] = ((tt * 128 + r) >= o).astype(bf)
    m["SPMASK"] = np.ascontiguousarray(sp.reshape(128, COLW))
    cm = np.zeros((128, 4, 512), np.float32)
    cvec = np.arange(512)[None, :]
    for k in range(4):
        cm[:, k, :] = np.where(cvec >= (128 * k + r), 0.0, NEG)
    m["CM4"] = np.ascontiguousarray(cm.reshape(128, 2048))
    return m


def kernel(**inputs):
    x = np.asarray(inputs["L0_attn_x"])
    nc = build_nc()
    in_maps = [_host_inputs(x[b], inputs) for b in range(8)]
    res = run_bass_kernel_spmd(nc, in_maps, core_ids=list(range(8)))
    out = np.stack([np.asarray(r["y"], dtype=np.float32) for r in res.results], axis=0)
    return out


# version 32
# speedup vs baseline: 1.0065x; 1.0065x over previous
import numpy as np
import ml_dtypes
import concourse.bass as bass
import bass_rust
import concourse.mybir as mybir
from concourse.bass_utils import run_bass_kernel_spmd

F32 = mybir.dt.float32
BF16 = mybir.dt.bfloat16
AF = mybir.ActivationFunctionType
ALU = mybir.AluOpType
AX = mybir.AxisListType

T = 2048
C = 1024
NT = 16
CH = 8
H = 16
HD = 64
NO = 44
OFF = list(range(33)) + [48, 64, 96, 128, 192, 256, 384, 512, 768, 1024, 1536]
INV = 0.125
NEG = -1.0e30
E = 16
FHD = 128
COLW = NO * 256


TRACE = {}
STOP_AFTER = None


class _StopEmit(Exception):
    pass


def tt_min(o):
    return max(0, (o - 127 + 127) // 128)


def build_nc(debug=False):
    nc = bass.Bass("TRN2", target_bir_lowering=False)
    d = {}

    def din(name, shape, dt):
        d[name] = nc.dram_tensor(name, list(shape), dt, kind="ExternalInput").ap()

    def dint(name, shape, dt):
        d[name] = nc.dram_tensor(name, list(shape), dt, kind="Internal").ap()

    din("x", (T, C), F32)
    din("qkv_wT", (C, 3 * C), F32)
    din("qw_q", (C, C), F32)
    din("gate_wT", (C, C), F32)
    din("out_wT", (C, C), F32)
    din("qb_fm", (128, 16), F32)
    din("vb_row", (1, C), F32)
    din("qbh", (64, 16), F32)
    din("gb_row", (1, C), F32)
    din("ob_row", (1, C), F32)
    din("se", (44, 64), F32)
    din("pb_row", (1, COLW), F32)
    din("ifg_row", (1, 16), F32)
    din("sim", (C, E), F32)
    din("gates_row", (1, E), F32)
    din("fq", (E, C, FHD), F32)
    din("fk", (E, C, FHD), F32)
    din("fv", (E, C, FHD), F32)
    din("fo", (E, FHD, C), F32)
    din("IDN", (128, 128), BF16)
    din("SPMASK", (128, COLW), BF16)
    din("CM4", (128, 4 * 512), F32)

    d["y"] = nc.dram_tensor("y", [T, C], F32, kind="ExternalOutput").ap()
    if debug:
        for nm, shape, dt in (("dbg_t1", (128, COLW), BF16), ("dbg_v", (T, C), BF16),
                              ("dbg_gate", (T, C), BF16), ("dbg_qfm", (128, 4096), BF16),
                              ("dbg_qk", (128, COLW), BF16),
                              ("dbg_lg", (128, COLW), F32), ("dbg_pm", (128, COLW), BF16),
                              ("dbg_pf", (128, COLW), BF16), ("dbg_x2", (T, C), F32),
                              ("dbg_rw", (128, NT * E), F32), ("dbg_wz", (128, NT * E), F32),
                              ("dbg_zc", (128, NT), F32), ("dbg_rn", (128, NT), F32),
                              ("dbg_of", (T, C), BF16),
                              ("dbg_z", (128, NT * 16), F32), ("dbg_zr", (128, NT * 16), F32),
                              ("dbg_w2", (128, NT * 16), F32), ("dbg_ifg", (128, 16), F32),
                              ("dbg_wse", (128, 5632), BF16), ("dbg_wtall", (1024, 4096), BF16),
                              ("dbg_wvg", (128, 16384), BF16)):
            d[nm] = nc.dram_tensor(nm, list(shape), dt, kind="ExternalOutput").ap()

    d["H2M"] = nc.dram_tensor("H2M", [128, CH * 16], BF16, kind="ExternalInput").ap()
    dint("wTallD", (C, 4096), BF16)
    dint("outwD", (C, C), BF16)
    dint("wQbfD", (E, C, 128), BF16)
    dint("wKbfD", (E, C, 128), BF16)
    dint("wVbfD", (E, C, 128), BF16)
    dint("oEbfD", (E, FHD, C), BF16)
    dint("PBbcD", (128, COLW), F32)
    dint("gateD", (T, C), BF16)
    dint("vD", (1536 + T, C), BF16)
    dint("x2D", (T, C), F32)
    dint("qsbD", (44, 16), F32)
    dint("snD", (1, E), F32)
    dint("zD", (1, T), F32)

    BIG = nc.alloc_sbuf_tensor("BIG", [128, 49152], BF16)
    SCR = nc.alloc_sbuf_tensor("SCR", [128, 16384], BF16)
    qk_all = nc.alloc_sbuf_tensor("qk_all", [128, COLW], BF16)
    t1_all = nc.alloc_sbuf_tensor("t1_all", [128, COLW], BF16)
    xt = nc.alloc_sbuf_tensor("xt", [128, 3072], F32)
    xnt = nc.alloc_sbuf_tensor("xnt", [128, 2 * C], BF16)
    idn = nc.alloc_sbuf_tensor("idn", [128, 128], BF16)
    h2 = nc.alloc_sbuf_tensor("h2", [128, CH * 16], BF16)
    ones_b = nc.alloc_sbuf_tensor("ones_b", [1, 128], BF16)
    ones_f = nc.alloc_sbuf_tensor("ones_f", [1, 128], F32)
    onecol_b = nc.alloc_sbuf_tensor("onecol_b", [128, 1], BF16)
    qbfm_s = nc.alloc_sbuf_tensor("qbfm_s", [128, 16], F32)
    vbrow = nc.alloc_sbuf_tensor("vbrow", [1, C], BF16)
    gbrow = nc.alloc_sbuf_tensor("gbrow", [1, C], BF16)
    obrow = nc.alloc_sbuf_tensor("obrow", [1, C], BF16)
    qsbrow = nc.alloc_sbuf_tensor("qsbrow", [1, 704], BF16)
    seb = nc.alloc_sbuf_tensor("seb", [128, 128], BF16)
    searr = nc.alloc_sbuf_tensor("searr", [128, 44], BF16)
    searr2 = nc.alloc_sbuf_tensor("searr2", [128, 88], BF16)
    qbh_s = nc.alloc_sbuf_tensor("qbh_s", [64, 16], BF16)
    ifgbc = nc.alloc_sbuf_tensor("ifgbc", [128, 16], F32)
    simb = nc.alloc_sbuf_tensor("simb", [128, CH * E], BF16)
    sim2 = nc.alloc_sbuf_tensor("sim2", [128, CH * E], F32)
    snrow = nc.alloc_sbuf_tensor("snrow", [1, E], F32)
    sgrow = nc.alloc_sbuf_tensor("sgrow", [1, E], F32)
    snbc = nc.alloc_sbuf_tensor("snbc", [128, E], F32)
    sgbc = nc.alloc_sbuf_tensor("sgbc", [128, E], F32)
    rn_all = nc.alloc_sbuf_tensor("rn_all", [128, NT], F32)
    rw_all = nc.alloc_sbuf_tensor("rw_all", [128, NT * E], F32)
    wz_all = nc.alloc_sbuf_tensor("wz_all", [128, NT * E], F32)
    zcol = nc.alloc_sbuf_tensor("zcol", [128, NT], F32)
    zrcol = nc.alloc_sbuf_tensor("zrcol", [128, NT], F32)
    stat = nc.alloc_sbuf_tensor("stat", [128, 32], F32)
    z_all = nc.alloc_sbuf_tensor("z_all", [128, NT * 16], F32)
    zr_all = nc.alloc_sbuf_tensor("zr_all", [128, NT * 16], F32)
    w2_all = nc.alloc_sbuf_tensor("w2_all", [128, NT * 16], F32)
    qsb44 = nc.alloc_sbuf_tensor("qsb44", [44, 16], F32)
    rowstage = nc.alloc_sbuf_tensor("rowstage", [128, 1024], F32)

    pm8 = nc.alloc_psum_tensor("pm8", [128, 3072], F32)
    pt1 = nc.alloc_psum_tensor("pt1", [128, 1024], BF16)
    pt2 = nc.alloc_psum_tensor("pt2", [128, 1024], BF16)

    def pt_(c):
        t_ = pt1 if (c % 2 == 0) else pt2
        return t_[:, (c // 2 % 4) * 128: (c // 2 % 4) * 128 + 128]

    def bank(k, w=512):
        return pm8[:, k * 512: k * 512 + w]

    semnames = ["s_dma", "s_dmw", "s_dvs", "s_dv2", "s_gp", "s_pe", "s_dve", "s_act", "s_pool", "s_st"]
    sems = {n: nc.alloc_semaphore(n) for n in semnames}

    def xnfm(c):
        return BIG[:, 32768 + c * 2048: 32768 + (c + 1) * 2048]

    def Qfm(c):
        return BIG[:, c * 2048: (c + 1) * 2048]

    def Kfm(c):
        return BIG[:, 16384 + c * 2048: 16384 + (c + 1) * 2048]

    def Kodd(c):
        return BIG[:, 32768 + c * 2048: 32768 + (c + 1) * 2048]

    def convf32():
        return BIG[:, 0:6144].bitcast(F32)

    def convbf():
        return BIG[:, 6144: 9216]

    def wse(c):
        return SCR[:, c * 704: (c + 1) * 704]

    def wcol(c):
        return SCR[:, 5632 + c * 256: 5632 + (c + 1) * 256]

    def wvg(c):
        if c < 5:
            return qk_all[:, c * 2048: (c + 1) * 2048]
        return SCR[:, 5632 + (c - 5) * 2048: 5632 + (c - 4) * 2048]

    def vstage():
        return SCR[:, 11776: 12800]

    def gstage():
        return SCR[:, 12800: 13824]

    def tmpc(c):
        return SCR[:, c * 2048: (c + 1) * 2048]

    def lgf32():
        return BIG[:, 0: 2 * COLW].bitcast(F32)

    def pbbuf(pi):
        return SCR[:, pi * 2816: (pi + 1) * 2816].bitcast(F32)

    def spbuf(pi):
        return SCR[:, 5632 + pi * 2816: 5632 + (pi + 1) * 2816]

    def outwT(c):
        return BIG[:, 24576 + c * 1024: 24576 + (c + 1) * 1024]

    def vsbuf(pi):
        return BIG[:, 16384 + pi * 1024: 16384 + (pi + 1) * 1024]

    def prodbuf(pi):
        return SCR[:, 4096 + pi * 1024: 4096 + (pi + 1) * 1024] if pi < 4 else \
            BIG[:, 22528 + (pi - 4) * 1024: 22528 + (pi - 3) * 1024]

    def ofbuf():
        return SCR[:, 8192: 9216]

    def gtbuf():
        return SCR[:, 9216: 10240]

    def ogbuf():
        return SCR[:, 10240: 11264]

    def ogfm():
        return SCR[:, 11264: 12288]

    def Vres(tt):
        return BIG[:, tt * 1024: (tt + 1) * 1024]

    def xn2fm(c):
        return BIG[:, c * 2048: (c + 1) * 2048]

    def expS(j):
        return BIG[:, 16384 + j * 512: 16384 + (j + 1) * 512]

    def scfm(qi, e):
        return BIG[:, 24576 + (qi * 16 + e) * 128: 24576 + (qi * 16 + e + 1) * 128]

    V2 = lambda qt: BIG[:, 32768 + qt * FHD: 32768 + (qt + 1) * FHD]
    q2fm = BIG[:, 34816: 36864]
    k2fm = BIG[:, 36864: 38912]
    aofm = BIG[:, 38912: 40960]
    aotok = lambda qt: BIG[:, 40960 + qt * FHD: 40960 + (qt + 1) * FHD]

    def webuf(pi):
        return SCR[:, pi * 3072: (pi + 1) * 3072]

    def rawsb(pi):
        return SCR[:, 9216 + pi * 384: 9216 + (pi + 1) * 384]

    def dgbuf(pi):
        return SCR[:, 9984 + pi * 128: 9984 + (pi + 1) * 128]

    def qkscr():
        return SCR[:, 10496: 10752]

    def causal():
        return SCR[:, 8192: 12288].bitcast(F32)

    def zrow():
        return SCR[0:1, 12288: 16384].bitcast(F32)

    def oebuf(pi):
        return SCR[:, pi * 1024: (pi + 1) * 1024]

    def lnsq():
        return xnt[:, :]

    def p7f(k):
        return SCR[:, 14000 + k * 32: 14000 + (k + 1) * 32].bitcast(F32)

    def walk(role, eng):
        cnt = {n: 0 for n in semnames}
        tr = TRACE.setdefault(role, [])

        def ckpt(name):
            if STOP_AFTER == name:
                raise _StopEmit()

        def tick(sem, inst, n=1):
            cnt[sem] += n
            if inst is not None:
                inst.then_inc(sems[sem], n)

        def on(r):
            return role == r

        def wt(sem, r):
            tr.append((sem, r, cnt[sem]))
            if role == r:
                eng.wait_ge(sems[sem], cnt[sem])

        def wtv(sem, r, val):
            tr.append((sem, r, val))
            if role == r and val > 0:
                eng.wait_ge(sems[sem], val)

        def vstep(emit):
            wt("s_dve", "vector")
            if on("vector"):
                tick("s_dve", emit(eng), 1)
            else:
                tick("s_dve", None, 1)

        def dma(ring, sem, out, in_, **kw):
            if on(ring):
                tick(sem, eng.dma_start(out=out, in_=in_, **kw), 16)
            else:
                tick(sem, None, 16)

        try:
            if on("sync"):
                tick("s_dma", eng.dma_start(out=idn[:, :], in_=d["IDN"]), 16)
                tick("s_dma", eng.dma_start(out=qbfm_s[:, :], in_=d["qb_fm"]), 16)
            else:
                tick("s_dma", None, 32)

            if on("gpsimd"):
                eng.memset(ones_b[:, :], 1.0)
                eng.memset(ones_f[:, :], 1.0)
                eng.memset(onecol_b[:, :], 1.0)
                eng.memset(seb[:, :], 0.0)
                eng.memset(searr2[:, :], 0.0)
                last = eng.memset(SCR[:, 15872:16384], 0.0)
                tick("s_pool", last, 1)
            else:
                tick("s_pool", None, 1)
            dma("gpsimd", "s_gp", h2[:, :], d["H2M"])

            for nm, dst, w in (("vb_row", vbrow, C), ("gb_row", gbrow, C), ("ob_row", obrow, C)):
                dma("sync", "s_dma", rowstage[0:1, 0:w], d[nm])
                wt("s_dma", "scalar")
                if on("scalar"):
                    tick("s_act", eng.copy(dst[:, :], rowstage[0:1, 0:w]), 1)
                else:
                    tick("s_act", None, 1)
                wt("s_act", "sync")
            dma("sync", "s_dma", rowstage[0:64, 0:16], d["qbh"])
            wt("s_dma", "scalar")
            if on("scalar"):
                tick("s_act", eng.copy(qbh_s[:, :], rowstage[0:64, 0:16]), 1)
            else:
                tick("s_act", None, 1)
            wt("s_act", "sync")
            dma("sync", "s_dma", rowstage[0:44, 0:64], d["se"])
            wt("s_dma", "scalar")
            wt("s_pool", "scalar")
            if on("scalar"):
                eng.copy(seb[0:44, 0:64], rowstage[0:44, 0:64])
                tick("s_act", eng.copy(seb[0:44, 64:128], rowstage[0:44, 0:64]), 1)
            else:
                tick("s_act", None, 1)
            wt("s_act", "tensor")
            wt("s_dma", "tensor")
            if on("tensor"):
                tick("s_pe", eng.transpose(pt1[:, 0:128], seb[:, :], idn[:, :]), 1)
            else:
                tick("s_pe", None, 1)
            wt("s_pe", "vector")
            if on("vector"):
                tick("s_dve", eng.tensor_copy(searr[:, :], pt1[:, 0:44]), 1)
            else:
                tick("s_dve", None, 1)
            snap_searr_cp = cnt["s_dve"]
            wt("s_dve", "scalar")
            wt("s_pool", "scalar")
            if on("scalar"):
                eng.copy(searr2[0:64, 0:44], searr[0:64, :])
                tick("s_act", eng.copy(searr2[64:128, 44:88], searr[64:128, :]), 1)
            else:
                tick("s_act", None, 1)

            for c in range(CH):
                wt("s_dve", "sync")
                dma("sync", "s_dma", rowstage[:, 0:16], d["sim"][c * 128:(c + 1) * 128, :])
                wt("s_dma", "vector")
                if on("vector"):
                    eng.tensor_copy(simb[:, c * E:(c + 1) * E], rowstage[:, 0:16])
                    tick("s_dve", eng.tensor_tensor(sim2[:, c * E:(c + 1) * E], rowstage[:, 0:16],
                                                    rowstage[:, 0:16], ALU.mult), 1)
                else:
                    tick("s_dve", None, 1)

            wt("s_dve", "sync")
            dma("sync", "s_dma", rowstage[0:1, 0:16], d["ifg_row"])
            wt("s_dma", "tensor")
            wt("s_pool", "tensor")
            if on("tensor"):
                tick("s_pe", eng.matmul(bank(4, 16), ones_f[:, :], rowstage[0:1, 0:16], start=True, stop=True), 1)
            else:
                tick("s_pe", None, 1)
            wt("s_pe", "scalar")
            if on("scalar"):
                tick("s_act", eng.copy(ifgbc[:, :], bank(4, 16)), 1)
            else:
                tick("s_act", None, 1)

            wt("s_dve", "tensor")
            wt("s_act", "tensor")
            if on("tensor"):
                tick("s_pe", eng.matmul(pm8[0:44, 4 * 512 + 16:4 * 512 + 32], searr[0:64, :], qbh_s[:, :],
                                        start=True, stop=True), 1)
            else:
                tick("s_pe", None, 1)
            wt("s_pe", "scalar")
            if on("scalar"):
                tick("s_act", eng.copy(qsb44[:, :], pm8[0:44, 4 * 512 + 16:4 * 512 + 32]), 1)
            else:
                tick("s_act", None, 1)
            wt("s_act", "sync")
            dma("sync", "s_dma", d["qsbD"], qsb44[:, :])
            dma("sync", "s_dma", rowstage[0:1, 0:704], d["qsbD"].rearrange("i h -> (i h)").unsqueeze(0))
            wt("s_dma", "scalar")
            if on("scalar"):
                tick("s_act", eng.copy(qsbrow[:, :], rowstage[0:1, 0:704]), 1)
            else:
                tick("s_act", None, 1)

            def ln_phase(src_d, dst_fm, with_l2, store_snaps=None, n_tiles=NT, sqscr=None,
                         dual_ring=False):
                if sqscr is None:
                    sqscr = SCR[:, 0:1024]
                snapA = {}; snapSq = {}; snapTr = {}; snapEv = {}; snapL2 = {}
                wt("s_pool", "scalar")
                for tt in range(n_tiles):
                    ring = "scalar" if (dual_ring and tt % 2 == 1) else "sync"
                    lsem = "s_dmw" if ring == "scalar" else "s_dma"
                    b = (tt % 2) * 12
                    xv = xt[:, (tt % 3) * C:(tt % 3 + 1) * C]
                    xns = xnt[:, (tt % 2) * C:(tt % 2 + 1) * C]
                    if tt >= 3:
                        wtv("s_dve", ring, snapA[tt - 3])
                        wtv("s_act", ring, snapSq[tt - 3])
                    if store_snaps is not None:
                        wtv("s_dma", ring, store_snaps[tt])
                    dma(ring, lsem, xv, src_d[tt * 128:(tt + 1) * 128, :])
                    ld = cnt[lsem]
                    wtv(lsem, "vector", ld)
                    if on("vector"):
                        tick("s_dve", eng.reduce_sum(stat[:, b:b + 1], xv, AX.X), 1)
                    else:
                        tick("s_dve", None, 1)
                    wtv(lsem, "scalar", ld)
                    if tt >= 2:
                        wtv("s_dve", "scalar", snapA[tt - 2])
                    if on("scalar"):
                        tick("s_act", eng.activation(sqscr, xv, AF.Square,
                                                     accum_out=stat[:, b + 1:b + 2]), 1)
                    else:
                        tick("s_act", None, 1)
                    snapSq[tt] = cnt["s_act"]
                    wt("s_act", "vector")
                    vstep(lambda e: e.tensor_scalar(stat[:, b + 2:b + 3], stat[:, b:b + 1], 1.0 / C, None, ALU.mult))
                    vstep(lambda e: e.tensor_tensor(stat[:, b + 3:b + 4], stat[:, b + 2:b + 3], stat[:, b + 2:b + 3], ALU.mult))
                    vstep(lambda e: e.scalar_tensor_tensor(stat[:, b + 4:b + 5], stat[:, b + 1:b + 2], 1.0 / C,
                                                           stat[:, b + 3:b + 4], ALU.mult, ALU.subtract))
                    wt("s_dve", "scalar")
                    if on("scalar"):
                        tick("s_act", eng.activation(stat[:, b + 5:b + 6], stat[:, b + 4:b + 5], AF.Sqrt,
                                                     bias=np.float32(1e-5)), 1)
                    else:
                        tick("s_act", None, 1)
                    wt("s_act", "vector")
                    vstep(lambda e: e.reciprocal(stat[:, b + 6:b + 7], stat[:, b + 5:b + 6]))
                    if tt >= 2:
                        wtv("s_pe", "vector", snapTr[tt - 2])
                        if with_l2:
                            wtv("s_act", "vector", snapL2[tt - 2])
                    vstep(lambda e: e.tensor_scalar(xns, xv, stat[:, b + 2:b + 3], stat[:, b + 6:b + 7],
                                                    ALU.subtract, ALU.mult))
                    snapA[tt] = cnt["s_dve"]
                    if with_l2:
                        wt("s_dve", "scalar")
                        if on("scalar"):
                            tick("s_act", eng.activation(SCR[:, 0:1024], xns, AF.Square,
                                                         accum_out=stat[:, b + 7:b + 8]), 1)
                        else:
                            tick("s_act", None, 1)
                        snapL2[tt] = cnt["s_act"]
                        wt("s_act", "scalar")
                        if on("scalar"):
                            tick("s_act", eng.activation(stat[:, b + 8:b + 9], stat[:, b + 7:b + 8], AF.Sqrt,
                                                         bias=np.float32(1e-12)), 1)
                        else:
                            tick("s_act", None, 1)
                        wt("s_act", "vector")
                        vstep(lambda e, t=tt: e.reciprocal(rn_all[:, t:t + 1], stat[:, b + 8:b + 9]))
                    evc = {}
                    for c in range(CH):
                        if c == 0:
                            wt("s_dve", "tensor")
                        if c >= 2:
                            wtv("s_act", "tensor", evc[c - 2])
                        elif tt >= 1:
                            wtv("s_act", "tensor", snapEv[tt - 1])
                        else:
                            wt("s_act", "tensor")
                        if on("tensor"):
                            tick("s_pe", eng.transpose(pt_(c),
                                                       xns[:, c * 128:(c + 1) * 128], idn[:, :]), 1)
                        else:
                            tick("s_pe", None, 1)
                        wt("s_pe", "scalar")
                        if on("scalar"):
                            tick("s_act", eng.copy(dst_fm(c)[:, tt * 128:(tt + 1) * 128],
                                                   pt_(c)), 1)
                        else:
                            tick("s_act", None, 1)
                        evc[c] = cnt["s_act"]
                    snapTr[tt] = cnt["s_pe"]
                    snapEv[tt] = cnt["s_act"]

            ckpt("P0")
            import os as _os
            snap_p0_act = cnt["s_act"]
            def cv32(sl):
                return SCR[:, sl * 2048:(sl + 1) * 2048].bitcast(F32)

            def cvbf(sl):
                return BIG[:, 16384 + sl * 1024: 16384 + (sl + 1) * 1024]

            wtv("s_dve", "tensor", snap_searr_cp)
            ln_phase(d["x"], xnfm, False, n_tiles=int(_os.environ.get("LN_NTILES", NT)),
                     sqscr=qk_all[:, 0:1024])
            snap_p1_dma = cnt["s_dma"]
            snap_ln1_act = cnt["s_act"]

            ckpt("P1")
            def qwb():
                return SCR[:, 13824: 14848]

            def pbstage():
                return SCR[:, 14848: 15872].bitcast(F32)

            def qwb8(c_):
                return SCR[:, 7680 + c_ * 1024: 7680 + (c_ + 1) * 1024] if c_ < 6 else \
                    SCR[:, 13824 + (c_ - 6) * 1024: 13824 + (c_ - 5) * 1024]

            snap_wse_mm = {}
            snap_wse_evac = {}
            wt("s_pool", "scalar")
            wt("s_pool", "gpsimd")
            for c in range(CH):
                dma("gpsimd", "s_gp", qwb8(c), d["qw_q"][c * 128:(c + 1) * 128, :])
            snap_qwb = cnt["s_gp"]
            wtv("s_gp", "tensor", snap_qwb)
            kw = 0
            for c in range(CH):
                for cs in range(CH):
                    b = 4 + (kw % 2)
                    if kw >= 2:
                        wtv("s_act", "tensor", snap_wse_evac[kw - 2])
                    else:
                        wtv("s_act", "tensor", snap_p0_act)
                    if on("tensor"):
                        tick("s_pe", eng.matmul(bank(b, 88),
                                                qwb8(c)[:, cs * 128:(cs + 1) * 128],
                                                searr2[:, :], start=True, stop=True), 1)
                    else:
                        tick("s_pe", None, 1)
                    snap_wse_mm[kw] = cnt["s_pe"]
                    wtv("s_pe", "scalar", snap_wse_mm[kw])
                    if on("scalar"):
                        wv = wse(cs).rearrange("p (i h) -> p i h", h=16)
                        eng.copy(wv[:, :, 2 * c], bank(b, 88)[:, 0:44])
                        tick("s_act", eng.copy(wv[:, :, 2 * c + 1], bank(b, 88)[:, 44:88]), 1)
                    else:
                        tick("s_act", None, 1)
                    snap_wse_evac[kw] = cnt["s_act"]
                    kw += 1
            snap_wse_pe = cnt["s_pe"]

            ckpt("WSE")

            ckpt("PBD")
            wtv("s_dma", "gpsimd", snap_p1_dma)
            for wnm, wdst in (("fq", "wQbfD"), ("fk", "wKbfD"), ("fv", "wVbfD")):
                for e in range(E):
                    dma("gpsimd", "s_gp", d[wdst][e], d[wnm][e])
            for e in range(E):
                dma("gpsimd", "s_gp", d["oEbfD"][e], d["fo"][e])
            snap_wffn = cnt["s_gp"]
            ckpt("FFN")
            dma("gpsimd", "s_gp", d["outwD"], d["out_wT"])
            snap_outw_cast = cnt["s_gp"]
            if debug:
                wt("s_act", "sync")
                for cs_ in range(CH):
                    dma("sync", "s_dma", d["dbg_wse"][:, cs_ * 704:(cs_ + 1) * 704], wse(cs_))
            ckpt("P0W")
            def p2af32(sl_):
                return (SCR[:, 7680:11776] if sl_ == 0 else SCR[:, 11776:15872]).bitcast(F32)

            wt("s_pool", "sync")
            snap_p2a_mm = {}
            snap_p2a_evac = {}
            snap_p2a_cast = {}
            for jg in range(8):
                js0 = jg * 2
                wf32 = p2af32(jg % 2)
                if jg >= 2:
                    wtv("s_dve", "sync", snap_p2a_cast[jg - 2])
                if jg >= 1:
                    wtv("s_pe", "sync", snap_p2a_mm[jg - 1])
                else:
                    wtv("s_pe", "sync", snap_wse_pe)
                for c in range(CH):
                    dma("sync", "s_dma", wf32[:, c * 256:(c + 1) * 256],
                        d["qkv_wT"][c * 128:(c + 1) * 128, js0 * 128: js0 * 128 + 256])
                wtv("s_dma", "vector", cnt["s_dma"])
                if jg >= 1:
                    wtv("s_pe", "vector", snap_p2a_mm[jg - 1])
                if on("vector"):
                    last = None
                    for c in range(CH):
                        last = eng.tensor_copy(wcol(c), wf32[:, c * 256:(c + 1) * 256])
                    tick("s_dve", last, 1)
                else:
                    tick("s_dve", None, 1)
                snap_p2a_cast[jg] = cnt["s_dve"]
                for jsl in range(2):
                    js = js0 + jsl
                    for c in range(CH):
                        if c == 0:
                            wtv("s_dve", "tensor", snap_p2a_cast[jg])
                            wtv("s_act", "tensor", snap_ln1_act)
                            if (jg, jsl) > (0, 0):
                                pj = (jg, jsl - 1) if jsl == 1 else (jg - 1, 1)
                                wtv("s_act", "tensor", snap_p2a_evac[pj])
                        for tg in range(4):
                            if on("tensor"):
                                tick("s_pe", eng.matmul(bank(tg),
                                                        wcol(c)[:, jsl * 128:(jsl + 1) * 128],
                                                        xnfm(c)[:, tg * 512:(tg + 1) * 512],
                                                        start=(c == 0), stop=(c == CH - 1)), 1)
                            else:
                                tick("s_pe", None, 1)
                    snap_p2a_mm[jg] = cnt["s_pe"]
                    for tg in range(4):
                        wt("s_pe", "scalar")
                        dstc = Qfm(js) if js < 8 else Kfm(js - 8)
                        if on("scalar"):
                            tick("s_act", eng.activation(dstc[:, tg * 512:(tg + 1) * 512], bank(tg),
                                                         AF.Identity, bias=qbfm_s[:, js:js + 1], scale=np.float32(1.0)), 1)
                        else:
                            tick("s_act", None, 1)
                    snap_p2a_evac[(jg, jsl)] = cnt["s_act"]

            ckpt("P2a")
            wt("s_pool", "tensor")
            wt("s_pool", "sync")
            for zz in range(12):
                dma("sync", "s_dma", d["vD"][zz * 128:(zz + 1) * 128, 0:512], SCR[:, 15872:16384])
                dma("sync", "s_dma", d["vD"][zz * 128:(zz + 1) * 128, 512:1024], SCR[:, 15872:16384])
            wt("s_pe", "sync")
            snap_wvg_cast = {}
            for c in range(CH):
                for half, srcap in ((0, d["qkv_wT"][c * 128:(c + 1) * 128, 2048:3072]),
                                    (1, d["gate_wT"][c * 128:(c + 1) * 128, :])):
                    k2_ = 2 * c + half
                    wf32 = (SCR[:, 11776:13824] if k2_ % 2 == 0 else SCR[:, 13824:15872]).bitcast(F32)
                    if k2_ < 2:
                        wtv("s_dve", "sync", snap_p2a_cast[7])
                    else:
                        wtv("s_dve", "sync", snap_wvg_cast[k2_ - 2])
                    dma("sync", "s_dma", wf32, srcap)
                    wtv("s_dma", "vector", cnt["s_dma"])
                    if on("vector"):
                        tick("s_dve", eng.tensor_copy(wvg(c)[:, half * 1024:(half + 1) * 1024], wf32), 1)
                    else:
                        tick("s_dve", None, 1)
                    snap_wvg_cast[k2_] = cnt["s_dve"]
            snap_wvg_dve = cnt["s_dve"]

            def vstage2(pi_):
                return SCR[:, 11776: 12800] if pi_ == 0 else SCR[:, 13824: 14848]

            def gstage2(pi_):
                return SCR[:, 12800: 13824] if pi_ == 0 else SCR[:, 14848: 15872]

            snap_p2b_st = {}
            for tt in range(NT):
                for c in range(CH):
                    if tt == 0:
                        wtv("s_dve", "tensor", snap_wvg_cast[2 * c + 1])
                    if c == 0:
                        wt("s_act", "tensor")
                    lhs = xnfm(c)[:, tt * 128:(tt + 1) * 128]
                    if on("tensor"):
                        tick("s_pe", eng.matmul(bank(0), lhs, wvg(c)[:, 0:512], start=(c == 0), stop=False), 1)
                        tick("s_pe", eng.matmul(bank(1), lhs, wvg(c)[:, 512:1024], start=(c == 0), stop=False), 1)
                        tick("s_pe", eng.matmul(bank(2), lhs, wvg(c)[:, 1024:1536], start=(c == 0), stop=False), 1)
                        tick("s_pe", eng.matmul(bank(3), lhs, wvg(c)[:, 1536:2048], start=(c == 0), stop=False), 1)
                        tick("s_pe", eng.matmul(bank(4), lhs, wse(c)[:, 0:512], start=(c == 0), stop=False), 1)
                        tick("s_pe", eng.matmul(bank(5, 192), lhs, wse(c)[:, 512:704], start=(c == 0), stop=False), 1)
                    else:
                        tick("s_pe", None, 6)
                if on("tensor"):
                    tick("s_pe", eng.matmul(bank(0), ones_b[:, :], vbrow[:, 0:512], start=False, stop=True), 1)
                    tick("s_pe", eng.matmul(bank(1), ones_b[:, :], vbrow[:, 512:1024], start=False, stop=True), 1)
                    tick("s_pe", eng.matmul(bank(2), ones_b[:, :], gbrow[:, 0:512], start=False, stop=True), 1)
                    tick("s_pe", eng.matmul(bank(3), ones_b[:, :], gbrow[:, 512:1024], start=False, stop=True), 1)
                    tick("s_pe", eng.matmul(bank(4), ones_b[:, :], qsbrow[:, 0:512], start=False, stop=True), 1)
                    tick("s_pe", eng.matmul(bank(5, 192), ones_b[:, :], qsbrow[:, 512:704], start=False, stop=True), 1)
                else:
                    tick("s_pe", None, 6)
                wt("s_pe", "scalar")
                if tt >= 2:
                    wtv("s_dma", "scalar", snap_p2b_st[tt - 2])
                if on("scalar"):
                    eng.copy(vstage2(tt % 2)[:, 0:512], bank(0))
                    eng.copy(vstage2(tt % 2)[:, 512:1024], bank(1))
                    eng.activation(gstage2(tt % 2)[:, 0:512], bank(2), AF.Sigmoid)
                    tick("s_act", eng.activation(gstage2(tt % 2)[:, 512:1024], bank(3), AF.Sigmoid), 1)
                    t1ap = t1_all[:, :].rearrange("p (o s) -> p o s", s=256)[:, :, tt * 16: tt * 16 + 16]
                    eng.activation(t1ap[:, 0:32, :],
                                   bank(4).rearrange("p (o h) -> p o h", h=16),
                                   AF.Identity, bias=np.float32(INV), scale=np.float32(INV))
                    tick("s_act", eng.activation(t1ap[:, 32:44, :],
                                                 bank(5, 192).rearrange("p (o h) -> p o h", h=16),
                                                 AF.Identity, bias=np.float32(INV), scale=np.float32(INV)), 1)
                else:
                    tick("s_act", None, 2)
                wt("s_act", "sync")
                dma("sync", "s_dma", d["vD"][1536 + tt * 128:1536 + (tt + 1) * 128, :], vstage2(tt % 2))
                dma("sync", "s_dma", d["gateD"][tt * 128:(tt + 1) * 128, :], gstage2(tt % 2))
                snap_p2b_st[tt] = cnt["s_dma"]

            if debug:
                wt("s_dve", "sync")
                for c_ in range(CH):
                    dma("sync", "s_dma", d["dbg_wvg"][:, c_ * 2048:(c_ + 1) * 2048], wvg(c_))
            ckpt("P2b")
            if debug:
                wt("s_act", "sync")
                wt("s_dma", "sync")
                dma("sync", "s_dma", d["dbg_t1"], t1_all[:, :])
                dma("sync", "s_dma", d["dbg_v"], d["vD"][1536:, :])
                dma("sync", "s_dma", d["dbg_gate"], d["gateD"])
                dma("sync", "s_dma", d["dbg_qfm"][:, 0:2048], Qfm(0))
                dma("sync", "s_dma", d["dbg_qfm"][:, 2048:4096], Kfm(0))
            for c in range(CH):
                wt("s_act", "gpsimd")
                wt("s_dma", "gpsimd")
                if on("gpsimd"):
                    tick("s_pool", eng.tensor_copy(Kodd(c)[:, 0:2047], Kfm(c)[:, 1:2048]), 1)
                else:
                    tick("s_pool", None, 1)

            wt("s_act", "vector")
            wt("s_dma", "vector")
            if on("vector"):
                eng.memset(bank(4, 512), 0.0)
                tick("s_dve", eng.memset(bank(5, 512), 0.0), 1)
            else:
                tick("s_dve", None, 1)
            snap_pe_half = {}
            snap_dve_half = {}
            snap_act_evac = {}
            def tmph(h_, c_):
                return SCR[:, (h_ * 8 + c_) * 1024: (h_ * 8 + c_ + 1) * 1024]

            for oi, o in enumerate(OFF):
                even = (o % 2 == 0)
                pb = 4 + (oi % 2)
                for hf in range(2):
                    glo = hf * 1024
                    ghi = glo + 1024
                    lo = max(o, glo)
                    if lo >= ghi:
                        snap_dve_half[(oi, hf)] = (cnt["s_dve"], cnt["s_pool"])
                        snap_pe_half[(oi, hf)] = snap_pe_half.get((oi, hf - 1), snap_pe_half.get((oi - 1, 1), 0))
                        continue
                    for c in range(CH):
                        meng = "vector"
                        msem = "s_dve"
                        if oi == 0 and hf == 0:
                            if c == 0:
                                wt("s_pe", "vector")
                                wt("s_pool", "vector")
                        else:
                            prev = (oi - 1, hf) if (oi - 1, hf) in snap_pe_half else None
                            if prev is not None and c == 0:
                                wtv("s_pe", meng, snap_pe_half[prev])
                        if on(meng):
                            if o == 0 or even:
                                inst = eng.tensor_tensor(tmph(hf, c)[:, lo - glo:1024],
                                                         Qfm(c)[:, lo:ghi],
                                                         Kfm(c)[:, lo - o:ghi - o], ALU.mult)
                            else:
                                lo2 = max(o + 1, glo)
                                inst = eng.tensor_tensor(tmph(hf, c)[:, lo2 - glo:1024],
                                                         Qfm(c)[:, lo2:ghi],
                                                         Kodd(c)[:, lo2 - o - 1:ghi - o - 1], ALU.mult)
                                if lo2 == o + 1:
                                    inst = eng.tensor_tensor(tmph(hf, c)[:, o - glo:o - glo + 1],
                                                             Qfm(c)[:, o:o + 1],
                                                             Kfm(c)[:, 0:1], ALU.mult)
                            tick(msem, inst, 1)
                        else:
                            tick(msem, None, 1)
                    snap_dve_half[(oi, hf)] = (cnt["s_dve"], cnt["s_pool"])
                    if oi >= 2:
                        wtv("s_act", "tensor", snap_act_evac[oi - 2])
                    elif oi == 0 and hf == 0:
                        wt("s_act", "tensor")
                    wtv("s_dve", "tensor", snap_dve_half[(oi, hf)][0])
                    wtv("s_pool", "tensor", snap_dve_half[(oi, hf)][1])
                    for tt in range(max(tt_min(o), hf * 8), (hf + 1) * 8):
                        for c in range(CH):
                            if on("tensor"):
                                tick("s_pe", eng.matmul(pm8[:, pb * 512 + tt * 16: pb * 512 + tt * 16 + 16],
                                                        tmph(hf, c)[:, (tt - hf * 8) * 128:(tt - hf * 8 + 1) * 128],
                                                        h2[:, c * 16:(c + 1) * 16],
                                                        start=(c == 0), stop=(c == CH - 1)), 1)
                            else:
                                tick("s_pe", None, 1)
                    snap_pe_half[(oi, hf)] = cnt["s_pe"]
                wt("s_pe", "scalar")
                if on("scalar"):
                    tick("s_act", eng.copy(qk_all[:, oi * 256:(oi + 1) * 256], bank(pb, 256)), 1)
                else:
                    tick("s_act", None, 1)
                snap_act_evac[oi] = cnt["s_act"]

            if debug:
                wt("s_act", "sync")
                dma("sync", "s_dma", d["dbg_qk"], qk_all[:, :])
            wt("s_act", "vector")
            if debug:
                wt("s_dma", "vector")
            if on("vector"):
                tick("s_dve", eng.tensor_tensor(lgf32(), qk_all[:, :], t1_all[:, :], ALU.mult), 1)
            else:
                tick("s_dve", None, 1)
            for s in range(8):
                wt("s_dve", "sync")
                bcap_ = bass_rust.AP(d["pb_row"].tensor, s * 1408,
                                     bass_rust.VecI64Pair([[0, 128], [1, 1408]]))
                dma("sync", "s_dma", pbbuf(s % 2), bcap_)
                wt("s_dma", "vector")
                if on("vector"):
                    tick("s_dve", eng.tensor_tensor(lgf32()[:, s * 1408:(s + 1) * 1408],
                                                    lgf32()[:, s * 1408:(s + 1) * 1408],
                                                    pbbuf(s % 2), ALU.add), 1)
                else:
                    tick("s_dve", None, 1)
            wt("s_dve", "vector")
            if on("vector"):
                tick("s_dve", eng.tensor_scalar(lgf32(), lgf32(), 30.0, None, ALU.min), 1)
            else:
                tick("s_dve", None, 1)
            wt("s_dve", "scalar")
            if on("scalar"):
                tick("s_act", eng.activation(t1_all[:, :], lgf32(), AF.Exp), 1)
            else:
                tick("s_act", None, 1)
            for s in range(4):
                wt("s_dve", "sync")
                dma("sync", "s_dma", spbuf(s % 2), d["SPMASK"][:, s * 2816:(s + 1) * 2816])
                wt("s_dma", "vector")
                wt("s_act", "vector")
                if on("vector"):
                    tick("s_dve", eng.tensor_tensor(qk_all[:, s * 2816:(s + 1) * 2816],
                                                    t1_all[:, s * 2816:(s + 1) * 2816],
                                                    spbuf(s % 2), ALU.mult), 1)
                else:
                    tick("s_dve", None, 1)
            for tt in range(NT):
                srcz = qk_all[:, :].rearrange("p (o s) -> p o s", s=256)[:, :, tt * 16: tt * 16 + 16]
                srcr = srcz.transpose([0, 2, 1])
                vstep(lambda e, s=srcr, t=tt: e.tensor_reduce(
                    z_all[:, t * 16:(t + 1) * 16].unsqueeze(2), s, AX.X, ALU.add))
                vstep(lambda e, t=tt: e.reciprocal(
                    zr_all[:, t * 16:(t + 1) * 16], z_all[:, t * 16:(t + 1) * 16]))
                vstep(lambda e, t=tt: e.tensor_tensor(
                    w2_all[:, t * 16:(t + 1) * 16], zr_all[:, t * 16:(t + 1) * 16],
                    ifgbc[:, :], ALU.mult))
            for tt in range(NT):
                src = qk_all[:, :].rearrange("p (o s) -> p o s", s=256)[:, :, tt * 16: tt * 16 + 16]
                dst = t1_all[:, :].rearrange("p (o s) -> p o s", s=256)[:, :, tt * 16: tt * 16 + 16]
                w2v = w2_all[:, tt * 16:(tt + 1) * 16].unsqueeze(1).broadcast_to([128, NO, 16])
                wt("s_dve", "vector")
                if on("vector"):
                    tick("s_dve", eng.tensor_tensor(dst, src, w2v, ALU.mult), 1)
                else:
                    tick("s_dve", None, 1)

            ckpt("P4")
            if debug:
                wt("s_dve", "sync")
                wt("s_act", "sync")
                dma("sync", "s_dma", d["dbg_lg"], lgf32())
                dma("sync", "s_dma", d["dbg_pm"], qk_all[:, :])
                dma("sync", "s_dma", d["dbg_pf"], t1_all[:, :])
            wt("s_dve", "gpsimd")
            if on("gpsimd"):
                for pi_ in range(6):
                    last = eng.memset(vsbuf(pi_), 0.0)
                tick("s_pool", last, 1)
            else:
                tick("s_pool", None, 1)
            wt("s_pool", "sync")
            wt("s_pool", "vector")
            wt("s_pool", "gpsimd")
            wt("s_dma", "scalar")
            wt("s_dve", "scalar")
            wt("s_act", "scalar")
            for tt in range(NT):
                dma("scalar", "s_dmw", Vres(tt), d["vD"][1536 + tt * 128:1536 + (tt + 1) * 128, :])
            snap_vres = cnt["s_dmw"]
            wtv("s_gp", "scalar", snap_outw_cast)
            for c in range(CH):
                dma("scalar", "s_dmw", outwT(c), d["outwD"][c * 128:(c + 1) * 128, :])
            snap_outw = cnt["s_dmw"]
            wtv("s_dmw", "sync", snap_vres)
            wtv("s_dmw", "vector", snap_vres)
            wtv("s_dmw", "gpsimd", snap_vres)
            PVD = 6
            kglob = [0]
            snap_mult = []
            snap_pvmm = []
            snap_ofevac = {}
            snap_resid = {}
            snap_x2st = {}
            snap_og = {}
            snap_tr = {}
            snap_ogfm = {}
            for tt in range(NT):
                t0 = tt * 128
                valid = [oi for oi, o in enumerate(OFF) if t0 + 127 >= o]
                for k, oi in enumerate(valid):
                    o = OFF[oi]
                    a = t0 - o
                    kg = kglob[0]
                    pi = kg % PVD
                    aligned = (a >= 0 and a % 128 == 0)
                    muleng = "gpsimd" if kg % 9 in (1, 3, 5, 7) else "vector"
                    ldw = None
                    if not aligned:
                        ring, lsem = ("sync", "s_dvs") if kg % 2 == 0 else ("scalar", "s_dv2")
                        if kg >= PVD:
                            s_, v_ = snap_mult[kg - PVD]
                            wtv(s_, ring, v_)
                        dma(ring, lsem, vsbuf(pi), d["vD"][1536 + a:1536 + a + 128, :])
                        ldw = (lsem, cnt[lsem])
                    pcol = t1_all[:, oi * 256 + tt * 16: oi * 256 + tt * 16 + 16]
                    pbv = pcol.unsqueeze(2).broadcast_to([128, 16, 64])
                    if ldw is not None:
                        wtv(ldw[0], muleng, ldw[1])
                    if kg >= PVD:
                        wtv("s_pe", muleng, snap_pvmm[kg - PVD])
                    vsrc = Vres(a // 128) if aligned else vsbuf(pi)
                    if on(muleng):
                        tick("s_pool" if muleng == "gpsimd" else "s_dve",
                             eng.tensor_tensor(prodbuf(pi).rearrange("p (h e) -> p h e", e=64),
                                               vsrc.rearrange("p (h e) -> p h e", e=64),
                                               pbv, ALU.mult), 1)
                    else:
                        tick("s_pool" if muleng == "gpsimd" else "s_dve", None, 1)
                    msem = "s_pool" if muleng == "gpsimd" else "s_dve"
                    snap_mult.append((msem, cnt[msem]))
                    wtv(msem, "tensor", cnt[msem])
                    if k == 0 and tt >= 1:
                        wtv("s_act", "tensor", snap_ofevac[tt - 1])
                    if on("tensor"):
                        tick("s_pe", eng.matmul(bank(0), idn[:, :], prodbuf(pi)[:, 0:512],
                                                start=(k == 0), stop=(k == len(valid) - 1)), 1)
                        tick("s_pe", eng.matmul(bank(1), idn[:, :], prodbuf(pi)[:, 512:1024],
                                                start=(k == 0), stop=(k == len(valid) - 1)), 1)
                    else:
                        tick("s_pe", None, 2)
                    snap_pvmm.append(cnt["s_pe"])
                    kglob[0] += 1
                wt("s_pe", "scalar")
                if tt >= 1:
                    wtv("s_dve", "scalar", snap_og[tt - 1])
                if on("scalar"):
                    eng.copy(ofbuf()[:, 0:512], bank(0))
                    tick("s_act", eng.copy(ofbuf()[:, 512:1024], bank(1)), 1)
                else:
                    tick("s_act", None, 1)
                snap_ofevac[tt] = cnt["s_act"]
                if debug:
                    wt("s_act", "sync")
                    dma("sync", "s_dma", d["dbg_of"][t0:t0 + 128, :], ofbuf())
                if tt >= 1:
                    wtv("s_dve", "sync", snap_og[tt - 1])
                dma("sync", "s_dma", gtbuf(), d["gateD"][t0:t0 + 128, :])
                wt("s_dma", "vector")
                wt("s_act", "vector")
                if tt >= 1:
                    wtv("s_pe", "vector", snap_tr[tt - 1])
                if on("vector"):
                    tick("s_dve", eng.tensor_tensor(ogbuf(), ofbuf(), gtbuf(), ALU.mult), 1)
                else:
                    tick("s_dve", None, 1)
                snap_og[tt] = cnt["s_dve"]
                evc5 = {}
                for c in range(CH):
                    if c == 0:
                        wt("s_dve", "tensor")
                    if c >= 2:
                        wtv("s_act", "tensor", evc5[c - 2])
                    elif tt >= 1:
                        wtv("s_act", "tensor", snap_ogfm[tt - 1])
                    else:
                        wt("s_act", "tensor")
                    if on("tensor"):
                        tick("s_pe", eng.transpose(pt_(c),
                                                   ogbuf()[:, c * 128:(c + 1) * 128], idn[:, :]), 1)
                    else:
                        tick("s_pe", None, 1)
                    wt("s_pe", "scalar")
                    if on("scalar"):
                        tick("s_act", eng.copy(ogfm()[:, c * 128:(c + 1) * 128],
                                               pt_(c)), 1)
                    else:
                        tick("s_act", None, 1)
                    evc5[c] = cnt["s_act"]
                snap_tr[tt] = cnt["s_pe"]
                snap_ogfm[tt] = cnt["s_act"]
                for c in range(CH):
                    if c == 0:
                        wt("s_act", "tensor")
                        wtv("s_dmw", "tensor", snap_outw)
                        if tt >= 1:
                            wtv("s_dve", "tensor", snap_resid[tt - 1])
                    if on("tensor"):
                        tick("s_pe", eng.matmul(bank(2), ogfm()[:, c * 128:(c + 1) * 128],
                                                outwT(c)[:, 0:512], start=(c == 0), stop=False), 1)
                        tick("s_pe", eng.matmul(bank(3), ogfm()[:, c * 128:(c + 1) * 128],
                                                outwT(c)[:, 512:1024], start=(c == 0), stop=False), 1)
                    else:
                        tick("s_pe", None, 2)
                if on("tensor"):
                    tick("s_pe", eng.matmul(bank(2), ones_b[:, :], obrow[:, 0:512], start=False, stop=True), 1)
                    tick("s_pe", eng.matmul(bank(3), ones_b[:, :], obrow[:, 512:1024], start=False, stop=True), 1)
                else:
                    tick("s_pe", None, 2)
                xv = xt[:, (tt % 2) * C:(tt % 2) * C + C]
                x2v = xt[:, ((tt + 1) % 2) * C:((tt + 1) % 2) * C + C]
                if tt >= 1:
                    wtv("s_dve", "sync", snap_resid[tt - 1])
                    wtv("s_dma", "sync", snap_x2st[tt - 1])
                dma("sync", "s_dma", xv, d["x"][t0:t0 + 128, :])
                snap_xload = cnt["s_dma"]
                wt("s_pe", "vector")
                wtv("s_dma", "vector", snap_xload)
                if on("vector"):
                    eng.tensor_tensor(x2v[:, 0:512], bank(2), xv[:, 0:512], ALU.add)
                    tick("s_dve", eng.tensor_tensor(x2v[:, 512:1024], bank(3), xv[:, 512:1024], ALU.add), 1)
                else:
                    tick("s_dve", None, 1)
                snap_resid[tt] = cnt["s_dve"]
                wtv("s_dve", "sync", snap_resid[tt])
                dma("sync", "s_dma", d["x2D"][t0:t0 + 128, :], x2v)
                snap_x2st[tt] = cnt["s_dma"]

            wt("s_dve", "gpsimd")
            wt("s_pool", "gpsimd")
            wt("s_pe", "gpsimd")
            for e in range(E):
                dst = qk_all[:, e * 1024:(e + 1) * 1024] if e < 11 else t1_all[:, (e - 11) * 1024:(e - 10) * 1024]
                dma("gpsimd", "s_gp", dst, d["oEbfD"][e])
            snap_oeall = cnt["s_gp"]
            ckpt("P5")
            if debug:
                wt("s_dma", "sync")
                dma("sync", "s_dma", d["dbg_x2"], d["x2D"])
                wt("s_dma", "tensor")
                wt("s_dma", "scalar")
            wt("s_dve", "scalar")
            wt("s_pool", "scalar")
            ln_phase(d["x2D"], xn2fm, True, store_snaps=snap_x2st)

            ckpt("P6")
            wt("s_pool", "vector")
            wt("s_pool", "scalar")
            if on("vector"):
                tick("s_dve", eng.memset(stat[:, 9:10], 1.0), 1)
            else:
                tick("s_dve", None, 1)
            wt("s_dve", "tensor")
            wt("s_act", "tensor")
            for c in range(CH):
                if on("tensor"):
                    tick("s_pe", eng.matmul(pm8[0:E, 4 * 512:4 * 512 + 1], sim2[:, c * E:(c + 1) * E],
                                            stat[:, 9:10], start=(c == 0), stop=(c == CH - 1)), 1)
                else:
                    tick("s_pe", None, 1)
            wt("s_pe", "scalar")
            if on("scalar"):
                tick("s_act", eng.activation(stat[0:E, 10:11], pm8[0:E, 4 * 512:4 * 512 + 1],
                                             AF.Sqrt, bias=np.float32(1e-12)), 1)
            else:
                tick("s_act", None, 1)
            wt("s_act", "vector")
            if on("vector"):
                tick("s_dve", eng.reciprocal(stat[0:E, 11:12], stat[0:E, 10:11]), 1)
            else:
                tick("s_dve", None, 1)
            wt("s_dve", "sync")
            dma("sync", "s_dma", d["snD"].rearrange("r e -> (r e)").unsqueeze(1), stat[0:E, 11:12])
            dma("sync", "s_dma", snrow[:, :], d["snD"])
            dma("sync", "s_dma", rowstage[0:1, 0:16], d["gates_row"])
            wt("s_dma", "scalar")
            if on("scalar"):
                tick("s_act", eng.activation(sgrow[:, :], rowstage[0:1, 0:16], AF.Sigmoid), 1)
            else:
                tick("s_act", None, 1)
            wt("s_act", "tensor")
            wt("s_dma", "tensor")
            if on("tensor"):
                tick("s_pe", eng.matmul(pm8[:, 4 * 512:4 * 512 + E], ones_f[:, :], snrow[:, :],
                                        start=True, stop=True), 1)
            else:
                tick("s_pe", None, 1)
            wt("s_pe", "scalar")
            if on("scalar"):
                tick("s_act", eng.copy(snbc[:, :], pm8[:, 4 * 512:4 * 512 + E]), 1)
            else:
                tick("s_act", None, 1)
            wt("s_act", "tensor")
            if on("tensor"):
                tick("s_pe", eng.matmul(pm8[:, 4 * 512:4 * 512 + E], ones_f[:, :], sgrow[:, :],
                                        start=True, stop=True), 1)
            else:
                tick("s_pe", None, 1)
            wt("s_pe", "scalar")
            if on("scalar"):
                tick("s_act", eng.copy(sgbc[:, :], pm8[:, 4 * 512:4 * 512 + E]), 1)
            else:
                tick("s_act", None, 1)

            for tt in range(NT):
                for c in range(CH):
                    wt("s_act", "tensor")
                    if on("tensor"):
                        tick("s_pe", eng.matmul(bank(5, E), xn2fm(c)[:, tt * 128:(tt + 1) * 128],
                                                simb[:, c * E:(c + 1) * E],
                                                start=(c == 0), stop=(c == CH - 1)), 1)
                    else:
                        tick("s_pe", None, 1)
                wt("s_pe", "scalar")
                if on("scalar"):
                    tick("s_act", eng.activation(p7f(0), bank(5, E), AF.Identity,
                                                 bias=np.float32(0.0), scale=rn_all[:, tt:tt + 1]), 1)
                else:
                    tick("s_act", None, 1)
                wt("s_act", "vector")
                lgg = p7f(0)
                vstep(lambda e: e.tensor_tensor(lgg, lgg, snbc[:, :], ALU.mult))
                vstep(lambda e: e.tensor_tensor(lgg, lgg, sgbc[:, :], ALU.subtract))
                gated = p7f(1)
                vstep(lambda e: e.tensor_scalar(gated, lgg, 0.0, None, ALU.max))
                maskp = p7f(2)
                vstep(lambda e: e.tensor_scalar(maskp, lgg, 0.0, None, ALU.is_gt))
                vstep(lambda e: e.reduce_sum(stat[:, 0:1], maskp, AX.X))
                vstep(lambda e: e.tensor_scalar(stat[:, 1:2], stat[:, 0:1], 0.5, None, ALU.is_lt))
                vstep(lambda e: e.tensor_reduce(stat[:, 2:3].unsqueeze(2), lgg.unsqueeze(1), AX.X, ALU.max))
                ge1 = p7f(3)
                vstep(lambda e: e.tensor_scalar(ge1, lgg, stat[:, 2:3], None, ALU.is_ge))
                lm = p7f(4)
                vstep(lambda e: e.scalar_tensor_tensor(lm, ge1, NEG, lgg, ALU.mult, ALU.add))
                vstep(lambda e: e.tensor_reduce(stat[:, 3:4].unsqueeze(2), lm.unsqueeze(1), AX.X, ALU.max))
                fbm = p7f(5)
                vstep(lambda e: e.tensor_scalar(fbm, lgg, stat[:, 3:4], None, ALU.is_ge))
                tm = p7f(6)
                vstep(lambda e: e.scalar_tensor_tensor(tm, fbm, stat[:, 1:2], maskp, ALU.mult, ALU.add))
                wt("s_dve", "scalar")
                if on("scalar"):
                    tick("s_act", eng.activation(p7f(7), p7f(1), AF.Exp), 1)
                else:
                    tick("s_act", None, 1)
                wt("s_act", "vector")
                ex = p7f(7)
                vstep(lambda e: e.tensor_tensor(ex, ex, p7f(6), ALU.mult))
                vstep(lambda e: e.reduce_sum(stat[:, 4:5], ex, AX.X))
                vstep(lambda e: e.reciprocal(stat[:, 5:6], stat[:, 4:5]))
                vstep(lambda e, t=tt: e.tensor_scalar(rw_all[:, t * E:(t + 1) * E], ex,
                                                      stat[:, 5:6], None, ALU.mult))

            ckpt("P7")
            snap_we = {}
            snap_diag = {}
            snap_rawevac = {}
            snap_dg = {}
            idx8 = [0]
            pend = [None]

            def emit_diag(ix, qi, e):
                wtv("s_dve", "tensor", snap_dg[ix])
                wtv("s_act", "tensor", snap_rawevac[ix])
                if on("tensor"):
                    tick("s_pe", eng.matmul(pm8[:, qi * 512: qi * 512 + 384],
                                            dgbuf(ix % 2), rawsb(ix % 2),
                                            start=(e == 0), stop=(e == E - 1)), 1)
                else:
                    tick("s_pe", None, 1)
                snap_diag[ix] = cnt["s_pe"]

            snap_weld = {}
            for g in range(4):
                for e in range(E):
                    pi = (g * E + e) % 3
                    eg = g * E + e
                    ring, lsem = ("sync", "s_dvs") if eg % 2 == 0 else ("scalar", "s_dv2")
                    if eg >= 3:
                        wtv("s_pe", ring, snap_we[eg - 3])
                    else:
                        wt("s_pe", ring)
                        wt("s_dve", ring)
                    wtv("s_gp", ring, snap_wffn)
                    for wi_, wsrc_ in enumerate(("wQbfD", "wKbfD", "wVbfD")):
                        dma(ring, lsem,
                            webuf(pi).rearrange("p (k w) -> p k w", k=CH)[:, :, wi_ * 128:(wi_ + 1) * 128],
                            d[wsrc_][e].rearrange("(k c) h -> c k h", c=128))
                    snap_weld[eg] = (lsem, cnt[lsem])
                    for qi in range(4):
                        qt = g * 4 + qi
                        ix = idx8[0]
                        rb = 4 + (ix % 2)
                        for c in range(CH):
                            if c == 0:
                                wtv(snap_weld[eg][0], "tensor", snap_weld[eg][1])
                                if ix >= 2:
                                    wtv("s_act", "tensor", snap_rawevac[ix - 2])
                                else:
                                    wt("s_act", "tensor")
                            if on("tensor"):
                                tick("s_pe", eng.matmul(pm8[:, rb * 512:rb * 512 + 384],
                                                        xn2fm(c)[:, qt * 128:(qt + 1) * 128],
                                                        webuf(pi)[:, c * 384:(c + 1) * 384],
                                                        start=(c == 0), stop=(c == CH - 1)), 1)
                            else:
                                tick("s_pe", None, 1)
                        snap_we[eg] = cnt["s_pe"]
                        wt("s_pe", "scalar")
                        if on("scalar"):
                            tick("s_act", eng.copy(rawsb(ix % 2), pm8[:, rb * 512:rb * 512 + 384]), 1)
                        else:
                            tick("s_act", None, 1)
                        snap_rawevac[ix] = cnt["s_act"]
                        if ix >= 2:
                            wtv("s_pe", "vector", snap_diag[ix - 2])
                        if on("vector"):
                            tick("s_dve", eng.tensor_scalar(dgbuf(ix % 2), idn[:, :],
                                                            rw_all[:, qt * E + e: qt * E + e + 1],
                                                            None, ALU.mult), 1)
                        else:
                            tick("s_dve", None, 1)
                        snap_dg[ix] = cnt["s_dve"]
                        if pend[0] is not None:
                            emit_diag(*pend[0])
                        pend[0] = (ix, qi, e)
                        idx8[0] += 1
                emit_diag(*pend[0])
                pend[0] = None
                for qi in range(4):
                    qt = g * 4 + qi
                    wt("s_pe", "scalar")
                    if on("scalar"):
                        eng.copy(qkscr()[:, 0:128], pm8[:, qi * 512: qi * 512 + 128])
                        eng.copy(qkscr()[:, 128:256], pm8[:, qi * 512 + 128: qi * 512 + 256])
                        tick("s_act", eng.copy(V2(qt), pm8[:, qi * 512 + 256: qi * 512 + 384]), 1)
                    else:
                        tick("s_act", None, 1)
                    for wi, dstv in ((0, q2fm), (1, k2fm)):
                        wt("s_act", "tensor")
                        if on("tensor"):
                            tick("s_pe", eng.transpose(pt_(wi),
                                                       qkscr()[:, wi * 128:(wi + 1) * 128], idn[:, :]), 1)
                        else:
                            tick("s_pe", None, 1)
                        wt("s_pe", "scalar")
                        if on("scalar"):
                            tick("s_act", eng.copy(dstv[:, qt * 128:(qt + 1) * 128],
                                                   pt_(wi)), 1)
                        else:
                            tick("s_act", None, 1)

            ckpt("P8")
            wt("s_pe", "sync")
            dma("sync", "s_dma", causal(), d["CM4"])
            FSC = float(1.0 / np.sqrt(128.0))
            for g in range(4):
                njt = 4 * g + 4
                for j in range(njt):
                    pb = 2 + (j % 2)
                    wt("s_act", "tensor")
                    if on("tensor"):
                        tick("s_pe", eng.matmul(pm8[:, pb * 512:(pb + 1) * 512],
                                                k2fm[:, j * 128:(j + 1) * 128],
                                                q2fm[:, g * 512:(g + 1) * 512],
                                                start=True, stop=True), 1)
                    else:
                        tick("s_pe", None, 1)
                    if j >= 4 * g:
                        kk = j - 4 * g
                        wt("s_pe", "vector")
                        wt("s_dma", "vector")
                        if on("vector"):
                            tick("s_dve", eng.tensor_tensor(pm8[:, pb * 512:(pb + 1) * 512],
                                                            pm8[:, pb * 512:(pb + 1) * 512],
                                                            causal()[:, kk * 512:(kk + 1) * 512],
                                                            ALU.add), 1)
                        else:
                            tick("s_dve", None, 1)
                    wt("s_pe", "scalar")
                    wt("s_dve", "scalar")
                    if on("scalar"):
                        tick("s_act", eng.activation(expS(j), pm8[:, pb * 512:(pb + 1) * 512],
                                                     AF.Exp, scale=np.float32(FSC)), 1)
                    else:
                        tick("s_act", None, 1)
                wt("s_act", "tensor")
                for j in range(njt):
                    if on("tensor"):
                        tick("s_pe", eng.matmul(pm8[0:1, 5 * 512:5 * 512 + 512], onecol_b[:, :],
                                                expS(j), start=(j == 0), stop=(j == njt - 1)), 1)
                    else:
                        tick("s_pe", None, 1)
                wt("s_pe", "scalar")
                if on("scalar"):
                    tick("s_act", eng.copy(zrow()[:, g * 512:(g + 1) * 512],
                                           pm8[0:1, 5 * 512:5 * 512 + 512]), 1)
                else:
                    tick("s_act", None, 1)
                for j in range(njt):
                    if on("tensor"):
                        tick("s_pe", eng.matmul(pm8[:, 4 * 512:4 * 512 + 512], V2(j),
                                                expS(j), start=(j == 0), stop=(j == njt - 1)), 1)
                    else:
                        tick("s_pe", None, 1)
                wt("s_pe", "scalar")
                if on("scalar"):
                    tick("s_act", eng.copy(aofm[:, g * 512:(g + 1) * 512],
                                           pm8[:, 4 * 512:4 * 512 + 512]), 1)
                else:
                    tick("s_act", None, 1)
            wt("s_act", "sync")
            dma("sync", "s_dma", d["zD"], zrow())
            dma("sync", "s_dma", zcol[:, :],
                d["zD"].rearrange("r (t p) -> (r p) t", p=128), allow_slow_non_contiguous=True)
            wt("s_dma", "vector")
            vstep(lambda e: e.reciprocal(zrcol[:, :], zcol[:, :]))
            for tt in range(NT):
                vstep(lambda e, t=tt: e.tensor_scalar(wz_all[:, t * E:(t + 1) * E],
                                                      rw_all[:, t * E:(t + 1) * E],
                                                      zrcol[:, t:t + 1], None, ALU.mult))
            for qt in range(NT):
                wt("s_act", "tensor")
                if on("tensor"):
                    tick("s_pe", eng.transpose(pt_(qt),
                                               aofm[:, qt * 128:(qt + 1) * 128], idn[:, :]), 1)
                else:
                    tick("s_pe", None, 1)
                wt("s_pe", "scalar")
                if on("scalar"):
                    tick("s_act", eng.copy(aotok(qt), pt_(qt)), 1)
                else:
                    tick("s_act", None, 1)

            ckpt("P9")
            ogroups = [[0, 1, 2], [3, 4, 5], [6, 7, 8], [9, 10, 11], [12, 13, 14], [15]]
            isc = [0]
            snap_scmm = {}
            snap_scevac = {}
            snap_scdg = {}
            for og in ogroups:
                for qi, qt in enumerate(og):
                    for e in range(E):
                        i = isc[0]
                        if i >= 4:
                            wtv("s_pe", "vector", snap_scmm[i - 4])
                        else:
                            wt("s_act", "vector")
                            wt("s_pe", "vector")
                        if on("vector"):
                            tick("s_dve", eng.tensor_scalar(dgbuf(i % 4), idn[:, :],
                                                            wz_all[:, qt * E + e: qt * E + e + 1],
                                                            None, ALU.mult), 1)
                        else:
                            tick("s_dve", None, 1)
                        snap_scdg[i] = cnt["s_dve"]
                        wtv("s_dve", "tensor", snap_scdg[i])
                        if i >= 2:
                            wtv("s_act", "tensor", snap_scevac[i - 2])
                        else:
                            wt("s_act", "tensor")
                        ptf = (pt1 if i % 2 == 0 else pt2).bitcast(F32)[:, 0:128]
                        if on("tensor"):
                            tick("s_pe", eng.matmul(ptf, aotok(qt),
                                                    dgbuf(i % 4), start=True, stop=True), 1)
                        else:
                            tick("s_pe", None, 1)
                        snap_scmm[i] = cnt["s_pe"]
                        wtv("s_pe", "scalar", snap_scmm[i])
                        if on("scalar"):
                            tick("s_act", eng.copy(scfm(qi, e), ptf), 1)
                        else:
                            tick("s_act", None, 1)
                        snap_scevac[i] = cnt["s_act"]
                        isc[0] += 1
                for e in range(E):
                    oev = qk_all[:, e * 1024:(e + 1) * 1024] if e < 11 else t1_all[:, (e - 11) * 1024:(e - 10) * 1024]
                    for qi, qt in enumerate(og):
                        if e == 0 and qi == 0:
                            wtv("s_gp", "tensor", snap_oeall)
                            wt("s_act", "tensor")
                        if on("tensor"):
                            tick("s_pe", eng.matmul(bank(2 * qi), scfm(qi, e),
                                                    oev[:, 0:512], start=(e == 0), stop=(e == E - 1)), 1)
                            tick("s_pe", eng.matmul(bank(2 * qi + 1), scfm(qi, e),
                                                    oev[:, 512:1024], start=(e == 0), stop=(e == E - 1)), 1)
                        else:
                            tick("s_pe", None, 2)
                for qi, qt in enumerate(og):
                    t0 = qt * 128
                    xv = xt[:, (qt % 2) * C:(qt % 2) * C + C]
                    x2v = xt[:, ((qt + 1) % 2) * C:((qt + 1) % 2) * C + C]
                    wt("s_dve", "sync")
                    wt("s_dma", "sync")
                    wt("s_st", "sync")
                    dma("sync", "s_dma", xv, d["x2D"][t0:t0 + 128, :])
                    wt("s_pe", "vector")
                    wt("s_dma", "vector")
                    if on("vector"):
                        eng.tensor_tensor(x2v[:, 0:512], bank(2 * qi), xv[:, 0:512], ALU.add)
                        tick("s_dve", eng.tensor_tensor(x2v[:, 512:1024], bank(2 * qi + 1),
                                                        xv[:, 512:1024], ALU.add), 1)
                    else:
                        tick("s_dve", None, 1)
                    wt("s_dve", "sync")
                    dma("sync", "s_st", d["y"][t0:t0 + 128, :], x2v)

            if debug:
                wt("s_dve", "sync")
                wt("s_act", "sync")
                dma("sync", "s_dma", d["dbg_rw"], rw_all[:, :])
                dma("sync", "s_dma", d["dbg_wz"], wz_all[:, :])
                dma("sync", "s_dma", d["dbg_zc"], zcol[:, :])
                dma("sync", "s_dma", d["dbg_rn"], rn_all[:, :])
                dma("sync", "s_dma", d["dbg_z"], z_all[:, :])
                dma("sync", "s_dma", d["dbg_zr"], zr_all[:, :])
                dma("sync", "s_dma", d["dbg_w2"], w2_all[:, :])
                dma("sync", "s_dma", d["dbg_ifg"], ifgbc[:, :])
                wt("s_dma", "sync")
        except _StopEmit:
            pass
        for s_ in semnames:
            eng.wait_ge(sems[s_], cnt[s_])

    with nc.Block() as blk:
        blk.sync(lambda e: walk("sync", e))
        blk.tensor(lambda e: walk("tensor", e))
        blk.vector(lambda e: walk("vector", e))
        blk.scalar(lambda e: walk("scalar", e))
        blk.gpsimd(lambda e: walk("gpsimd", e))

    return nc


def _host_inputs(xb, w):
    bf = ml_dtypes.bfloat16
    m = {}
    m["x"] = np.ascontiguousarray(xb, dtype=np.float32)
    qkv_w = np.asarray(w["L0_attn_qkv_w"], dtype=np.float32)
    m["qkv_wT"] = np.ascontiguousarray(qkv_w.T)
    m["qw_q"] = np.ascontiguousarray(qkv_w[0:C])
    m["gate_wT"] = np.ascontiguousarray(np.asarray(w["L0_attn_gate_w"], np.float32).T)
    m["out_wT"] = np.ascontiguousarray(np.asarray(w["L0_attn_out_w"], np.float32).T)
    qb = np.asarray(w["L0_attn_qkv_b"], np.float32)
    m["qb_fm"] = np.ascontiguousarray(qb[0:2048].reshape(16, 128).T)
    m["vb_row"] = np.ascontiguousarray(qb[2048:3072].reshape(1, C))
    m["qbh"] = np.ascontiguousarray(qb[0:1024].reshape(16, 64).T)
    m["gb_row"] = np.ascontiguousarray(np.asarray(w["L0_attn_gate_b"], np.float32).reshape(1, C))
    m["ob_row"] = np.ascontiguousarray(np.asarray(w["L0_attn_out_b"], np.float32).reshape(1, C))
    m["se"] = np.ascontiguousarray(np.asarray(w["L0_attn_scale_embed"], np.float32))
    pb = np.asarray(w["L0_attn_pos_bias"], np.float32)
    pbrow = np.empty((NO, 16, 16), np.float32)
    pbrow[:, :, :] = pb[:, None, :]
    m["pb_row"] = np.ascontiguousarray(pbrow.reshape(1, COLW))
    m["ifg_row"] = np.ascontiguousarray(np.asarray(w["L0_attn_if_gain"], np.float32).reshape(1, 16))
    m["sim"] = np.ascontiguousarray(np.asarray(w["L0_ffn_sim_matrix"], np.float32))
    m["gates_row"] = np.ascontiguousarray(np.asarray(w["L0_ffn_gates"], np.float32).reshape(1, E))
    m["fq"] = np.ascontiguousarray(np.asarray(w["L0_ffn_q_proj"], np.float32))
    m["fk"] = np.ascontiguousarray(np.asarray(w["L0_ffn_k_proj"], np.float32))
    m["fv"] = np.ascontiguousarray(np.asarray(w["L0_ffn_v_proj"], np.float32))
    m["fo"] = np.ascontiguousarray(np.asarray(w["L0_ffn_o_proj"], np.float32))
    m["IDN"] = np.eye(128, dtype=bf)
    h2m = np.zeros((128, 8 * 16), bf)
    for c in range(8):
        h2m[0:64, c * 16 + 2 * c] = 1
        h2m[64:128, c * 16 + 2 * c + 1] = 1
    m["H2M"] = h2m
    r = np.arange(128)[:, None]
    sp = np.zeros((128, NO, 16, 16), bf)
    for oi, o in enumerate(OFF):
        for tt in range(16):
            sp[:, oi, tt, :] = ((tt * 128 + r) >= o).astype(bf)
    m["SPMASK"] = np.ascontiguousarray(sp.reshape(128, COLW))
    cm = np.zeros((128, 4, 512), np.float32)
    cvec = np.arange(512)[None, :]
    for k in range(4):
        cm[:, k, :] = np.where(cvec >= (128 * k + r), 0.0, NEG)
    m["CM4"] = np.ascontiguousarray(cm.reshape(128, 2048))
    return m


def kernel(**inputs):
    x = np.asarray(inputs["L0_attn_x"])
    nc = build_nc()
    in_maps = [_host_inputs(x[b], inputs) for b in range(8)]
    res = run_bass_kernel_spmd(nc, in_maps, core_ids=list(range(8)))
    out = np.stack([np.asarray(r["y"], dtype=np.float32) for r in res.results], axis=0)
    return out
